# version 13
# baseline (speedup 1.0000x reference)
import numpy as np
import ml_dtypes
import concourse.bass as bass
import concourse.mybir as mybir
from concourse.bass_utils import run_bass_kernel_spmd

F32 = mybir.dt.float32
BF16 = mybir.dt.bfloat16
Alu = mybir.AluOpType
Act = mybir.ActivationFunctionType
AxX = mybir.AxisListType.X
BF = ml_dtypes.bfloat16
AP = bass.AP

DM = 2048
DIN = 4096
NH = 64
HD = 64
NG = 32
CONV = 4352
DPROJ = 8512
NKD = DM // 128
NKI = DIN // 128
EPS = 1e-5
MAGIC = 256.0


class Ctr:

    def __init__(self, nc, name):
        self.h = nc.alloc_semaphore(name)
        self.n = 0

    def bump(self, k=1):
        self.n += k
        return self.n


class _DmaChain:

    def __init__(self, e, sem):
        self._e = e
        self._sem = sem
        self._n = 0

    def dma_start(self, *a, **kw):
        if self._n:
            self._e.wait_ge(self._sem, 16 * self._n)
        self._n += 1
        return self._e.dma_start(*a, **kw)

    def __getattr__(self, k):
        return getattr(self._e, k)


class Builder:
    def __init__(self, S, n_cores=8, debug_dump=False):
        assert S % 256 == 0
        self.S = S
        self.n_cores = n_cores
        self.SSL = []
        o = 0
        while o < S:
            w = min(512, S - o)
            self.SSL.append((o, w))
            o += w
        self.NCH = S // 128
        self.SH = S + 128
        self.debug_dump = debug_dump
        self.nc = bass.Bass("TRN2", target_bir_lowering=False, debug=False,
                            detect_race_conditions=False)
        self.ops = []
        self._declare_dram()
        self._alloc_sbuf()
        pa = self.nc.alloc_psum_tensor
        self.ps0 = pa("ps0", [128, 2048], F32)
        self.ps1 = pa("ps1", [128, 2048], F32)
        nc = self.nc
        self.qS = Ctr(nc, "qS")
        self.qA = Ctr(nc, "qA")
        self.qG = Ctr(nc, "qG")
        self.tS = Ctr(nc, "tS")
        self.vS = Ctr(nc, "vS")
        self.sS = Ctr(nc, "sS")
        self.gS = Ctr(nc, "gS")
        self.cS = Ctr(nc, "cS")

    def te(self, fn): self.ops.append(("tensor", fn))
    def ve(self, fn): self.ops.append(("vector", fn))
    def se(self, fn): self.ops.append(("scalar", fn))
    def sy(self, fn): self.ops.append(("sync", fn))
    def gp(self, fn): self.ops.append(("gpsimd", fn))

    def barrier(self, eng="sync"):
        w = [(self.qS.h, self.qS.n), (self.qA.h, self.qA.n), (self.qG.h, self.qG.n),
             (self.tS.h, self.tS.n), (self.vS.h, self.vS.n), (self.sS.h, self.sS.n)]

        def fn(e, w=w):
            for h, n in w:
                if n:
                    e.wait_ge(h, n)
        self.ops.append((eng, fn))

    def _declare_dram(self):
        nc, S, NCH, SH = self.nc, self.S, self.NCH, self.SH
        dt = nc.dram_tensor
        self.x_h = dt("x", [DM, SH], F32, kind="ExternalInput")
        self.w_in_h = dt("w_in", [DM, DPROJ], F32, kind="ExternalInput")
        self.conv_w_h = dt("conv_w", [CONV, 4], F32, kind="ExternalInput")
        self.conv_b_h = dt("conv_b", [CONV], F32, kind="ExternalInput")
        self.a_h = dt("a_vec", [NH], F32, kind="ExternalInput")
        self.d_h = dt("d_vec", [NH], F32, kind="ExternalInput")
        self.dtb_h = dt("dt_bias", [NH], F32, kind="ExternalInput")
        self.nw_h = dt("norm_w", [DIN], F32, kind="ExternalInput")
        self.w_out_h = dt("w_out", [DIN, DM], F32, kind="ExternalInput")
        self.wq_h = dt("wq_t", [DM, DM], F32, kind="ExternalInput")
        self.ident_h = dt("c_ident", [128, 128], F32, kind="ExternalInput")
        self.tril_h = dt("c_tril", [128, 128], BF16, kind="ExternalInput")
        self.inde_h = dt("c_inde", [64, NH * 128], BF16, kind="ExternalInput")
        self.sel_h = dt("c_sel", [2, 128], BF16, kind="ExternalInput")
        self.nwp_h = dt("c_nwp", [128, NKI], F32, kind="ExternalInput")
        self.cbp_h = dt("c_cbp", [128, 34], F32, kind="ExternalInput")
        self.cwp_h = dt("c_cwp", [128, 34 * 4], F32, kind="ExternalInput")
        self.dpk_h = dt("c_dpk", [2, NG], F32, kind="ExternalInput")
        self.pm_h = dt("c_pm", [128, 16], F32, kind="ExternalInput")
        self.out_h = dt("out", [DM, S], F32, kind="ExternalOutput")
        self.sp_xn = dt("sp_xn", [NKD, 128, S], F32, kind="Internal")
        self.sp_xc = dt("sp_xc", [34, 128, S], BF16, kind="Internal")
        self.sp_wq = dt("sp_wq", [DM, DM], BF16, kind="Internal")
        self.sp_y = dt("sp_y", [NCH, 128, DIN], BF16, kind="Internal")
        self.sp_yn = dt("sp_yn", [NKI, 128, S], BF16, kind="Internal")
        self.sp_ffn = dt("sp_ffn", [NKD, 128, S], BF16, kind="Internal")
        self.sp_rr = dt("sp_rr", [NCH, 2, NH * 128], BF16, kind="Internal")
        self.sp_ee = dt("sp_ee", [NCH, NH * 128], BF16, kind="Internal")
        self.sp_da = dt("sp_da", [2, NH, S], F32, kind="Internal")
        self.sp_cd = dt("sp_cd", [NH], F32, kind="Internal")
        self.sp_as = dt("sp_as", [S], F32, kind="Internal")
        self.ag_i = dt("ag_i", [128, DIN + NG], F32, kind="Internal")
        self.ag_o = dt("ag_o", [128 * self.n_cores, DIN + NG], F32, kind="Internal")
        self.dbg = {}
        if self.debug_dump:
            for name, shape, dty in [
                ("dbg_xn", [NKD, 128, S], BF16), ("dbg_dt", [2 * NH, S], F32),
                ("dbg_xc", [34, 128, S], BF16), ("dbg_y", [NCH, 128, DIN], BF16),
                ("dbg_yn", [NKI, 128, S], BF16), ("dbg_ffn", [NKD, 128, S], BF16),
                ("dbg_sr", [128, DIN + NG], F32),
                ("dbg_wq", [DM, DM], BF16), ("dbg_xnf", [NKD, 128, S], F32),
                ("dbg_rows", [97, SH], F32), ("dbg_wsp", [128, 12], F32),
                ("dbg_da", [2, NH, S], F32), ("dbg_ee", [NCH, NH * 128], BF16),
                ("dbg_rr", [NCH, 2, NH * 128], BF16),
                ("dbg_xq", [NKD, 128, S], BF16), ("dbg_sf", [128, S], F32),
                ("dbg_gt", [2, 128, DIN], BF16), ("dbg_cbm", [128, 128], BF16),
                ("dbg_ep", [128, DIN], BF16),
            ]:
                self.dbg[name] = dt(name, shape, dty, kind="ExternalOutput")

    def _alloc_sbuf(self):
        nc, S, SH = self.nc, self.S, self.SH
        sb = nc.alloc_sbuf_tensor
        self.c_id = sb("c_id", [128, 128], F32)
        self.c_trl = sb("c_trl", [128, 128], BF16)
        self.sel_bf = sb("sel_bf", [2, 128], BF16)
        self.pm_sb = sb("pm_sb", [128, 16], F32)
        self.ones_bf = sb("ones_bf", [128, 128], BF16)
        self.ones_f = sb("ones_f", [128, 128], F32)
        self.a_col = sb("a_col", [NH, 1], F32)
        self.d_col = sb("d_col", [NH, 1], F32)
        self.dtb_col = sb("dtb_col", [NH, 1], F32)
        self.nw_sb = sb("nw_sb", [128, NKI], F32)
        self.cw_sb = sb("cw_sb", [128, 34 * 4], F32)
        self.cb_sb = sb("cb_sb", [128, 34], F32)
        self.d_hp = sb("d_hp", [128, NG], F32)
        self.rows = sb("rows", [97, SH], F32)
        self.wsp = sb("wsp", [128, 12], F32)
        self.s_run = sb("s_run", [128, DIN], F32)
        self.p_run = sb("p_run", [128, NG], F32)
        self.cdexp = sb("cdexp", [128, NG], F32)
        self.e_tok = sb("e_tok", [128, NH], F32)
        self.e2_tok = sb("e2_tok", [128, NH], BF16)
        self.sml = sb("sml", [128, 64], F32)
        self.dacs_c = sb("dacs_c", [NH, 128], F32)
        self.cp_c = sb("cp_c", [NH, 128], F32)
        self.scr64 = sb("scr64", [NH, 128], F32)
        self.e2w = sb("e2w", [NH, 128], F32)
        self.p4f2 = sb("p4f2", [128, max(S // 2, 128)], BF16)
        self.p4x2 = sb("p4x2", [128, max(S // 2, 128)], F32)
        self.rhl = sb("rhl", [128, 128], BF16)
        self.d_lhA = sb("d_lhA", [66, 128], BF16)
        self.d_lhB = sb("d_lhB", [64, 128], BF16)
        self.xnt = sb("xnt", [128, NKD * S], BF16)
        self.xnt_h = sb("xnt_h", [128, NKD * 128], BF16)
        self.fsh = sb("fsh", [128, max(NKD * S, 32768)], BF16)
        self.gsh = sb("gsh", [128, 18816], BF16)
        self.kc = sb("kc", [128, 6], F32)
        nc.const_aps.register(F32, 30.0, self.kc.ap()[:, 0:1])
        nc.const_aps.register(F32, 1.0, self.kc.ap()[:, 1:2])
        nc.const_aps.register(F32, -1.0, self.kc.ap()[:, 2:3])
        nc.const_aps.register(F32, 0.0, self.kc.ap()[:, 3:4])
        nc.const_aps.register(F32, 12582912.0, self.kc.ap()[:, 4:5])
        self.sbuf_left = nc.sbuf_bytes_remaining
        assert self.sbuf_left >= 0, f"SBUF over by {-self.sbuf_left}"

    def fvw(self, off_b, n, dtype=BF16):
        assert off_b % 4 == 0
        if dtype == BF16:
            return self.fsh[:, off_b // 2: off_b // 2 + n]
        return self.fsh[:, off_b // 2: off_b // 2 + 2 * n].bitcast(F32)

    def gv(self, off_b, n, dtype=BF16):
        assert off_b % 4 == 0
        if dtype == BF16:
            return self.gsh[:, off_b // 2: off_b // 2 + n]
        return self.gsh[:, off_b // 2: off_b // 2 + 2 * n].bitcast(F32)

    @property
    def f_gt(self): return self.fvw(0, NG * 128)
    @property
    def f_xtok(self): return self.fvw(8192, DIN)
    @property
    def f_xe2(self): return self.fvw(16384, DIN)
    @property
    def f_prevs(self): return self.fvw(24576, DIN)
    @property
    def f_epat(self): return self.fvw(32768, DIN)
    @property
    def f_btok(self): return self.fvw(40960, 128)
    @property
    def f_drhs(self): return self.fvw(41216, NH * 128)[0:66, :]

    def emit_all(self):
        self.p_setup()
        self.p0_wscale()
        self.barrier("vector"); self.barrier("tensor"); self.barrier("sync")
        self.p1_layernorm()
        self.barrier("tensor"); self.barrier("gpsimd"); self.barrier("sync")
        self.p2a_inproj_conv()
        self.p1_ascale()
        self.barrier("vector"); self.barrier("tensor"); self.barrier("sync")
        self.barrier("scalar")
        self.p3a_states()
        self.p_ag_start()
        self.barrier("vector"); self.barrier("tensor"); self.barrier("sync")
        self.p5q_quant()
        self.barrier("vector"); self.barrier("tensor"); self.barrier("sync")
        self.p5m_ffn()
        self.p_ag_combine()
        self.barrier("vector"); self.barrier("tensor"); self.barrier("sync")
        self.barrier("scalar")
        self.p3b_ssd()
        self.barrier("vector"); self.barrier("tensor"); self.barrier("sync")
        self.barrier("gpsimd"); self.barrier("scalar")
        self.p2b_gate()
        self.barrier("vector"); self.barrier("tensor"); self.barrier("sync")
        self.barrier("gpsimd")
        self.p4_outproj()

    def p_setup(self):
        qS, vS, tS = self.qS, self.vS, self.tS
        q1 = qS.bump(16 * 12)

        def loads(sync, q1=q1):
            for o_, i_ in [
                (self.c_id[:, :], self.ident_h.ap()),
                (self.c_trl[:, :], self.tril_h.ap()),
                (self.sel_bf[:, :], self.sel_h.ap()),
                (self.pm_sb[:, :], self.pm_h.ap()),
                (self.a_col[:, :], self.a_h.ap().unsqueeze(1)),
                (self.d_col[:, :], self.d_h.ap().unsqueeze(1)),
                (self.dtb_col[:, :], self.dtb_h.ap().unsqueeze(1)),
                (self.nw_sb[:, :], self.nwp_h.ap()),
                (self.cw_sb[:, :], self.cwp_h.ap()),
                (self.cb_sb[:, :], self.cbp_h.ap()),
                (self.sml[0:1, 0:NG], self.dpk_h.ap()[0:1, :]),
                (self.sml[32:33, 0:NG], self.dpk_h.ap()[1:2, :]),
            ]:
                sync.dma_start(out=o_, in_=i_).then_inc(qS.h, 16)
        self.sy(loads)

        v1 = vS.bump()

        def inits(v, q1=q1, v1=v1):
            v.wait_ge(qS.h, q1)
            v.memset(self.ones_bf[:, :], 1.0)
            v.memset(self.ones_f[:, :], 1.0)
            v.memset(self.kc[:, 0:1], 30.0)
            v.memset(self.kc[:, 1:2], 1.0)
            v.memset(self.kc[:, 2:3], -1.0)
            v.memset(self.kc[:, 3:4], 0.0)
            v.memset(self.kc[:, 4:5], 12582912.0)
            if self.debug_dump:
                v.memset(self.rows[:, :], 0.0)
                v.memset(self.wsp[:, :], 0.0)
                v.memset(self.sml[:, :], 0.0)
            v.memset(self.d_lhA[64:66, :], 1.0)
            v.memset(self.s_run[:, :], 0.0)
            v.memset(self.p_run[:, :], 1.0).then_inc(vS.h, 1)
        self.ve(inits)

        t1 = tS.bump()

        def dhp_mm(t, v1=v1, t1=t1):
            t.wait_ge(vS.h, v1)
            t.matmul(self.ps1[0:64, 0:NG], self.ones_f[0:1, 0:64], self.sml[0:1, 0:NG],
                     start=True, stop=True)
            t.matmul(self.ps1[64:128, 0:NG], self.ones_f[32:33, 0:64],
                     self.sml[32:33, 0:NG], start=True, stop=True).then_inc(tS.h, 1)
        self.te(dhp_mm)

        v2 = vS.bump()

        def dhp_evac(v, t1=t1):
            v.wait_ge(tS.h, t1)
            v.tensor_copy(self.d_hp[:, :], self.ps1[:, 0:NG]).then_inc(vS.h, 1)
        self.ve(dhp_evac)

    def p0_wscale(self):
        qS, vS, tS, gS = self.qS, self.vS, self.tS, self.gS
        bufs = [self.gv(0, 2048, F32), self.gv(8192, 2048, F32)]
        acc = self.wsp[:, 0:1]
        ve_done = {}
        for i in range(NKD):
            buf = bufs[i % 2]
            wv = ve_done.get(i - 2)
            qt = qS.bump(16)

            def ld(sync, i=i, buf=buf, wv=wv):
                if wv:
                    sync.wait_ge(vS.h, wv)
                sync.dma_start(out=buf,
                               in_=self.wq_h.ap()[i * 128:(i + 1) * 128, :]).then_inc(qS.h, 16)
            self.sy(ld)
            vt = vS.bump()

            def red(v, i=i, buf=buf, qt=qt):
                v.wait_ge(qS.h, qt)
                dst = self.sml[:, 8 + (i % 2):9 + (i % 2)]
                v.tensor_reduce(dst, buf, AxX, Alu.add, apply_absolute_value=True)
                v.drain()
                if i == 0:
                    v.tensor_copy(acc, dst).then_inc(vS.h, 1)
                else:
                    v.tensor_add(acc, acc, dst).then_inc(vS.h, 1)
            self.ve(red)
            ve_done[i] = vt

        vall = vS.n
        t1 = tS.bump()

        def pred(t, vall=vall):
            t.wait_ge(vS.h, vall)
            t.matmul(self.ps1[0:1, 64:65], self.ones_f[:, 0:1], acc,
                     start=True, stop=True).then_inc(tS.h, 1)
        self.te(pred)
        v1 = vS.bump()

        def wsfin(v, t1=t1):
            v.wait_ge(tS.h, t1)
            v.tensor_scalar(self.wsp[0:1, 1:2], self.ps1[0:1, 64:65], 1.0 / (DM * DM), 1e-6,
                            Alu.mult, Alu.max)
            v.drain()
            v.reciprocal(self.wsp[0:1, 2:3], self.wsp[0:1, 1:2]).then_inc(vS.h, 1)
        self.ve(wsfin)
        t2b = tS.bump()

        def winv_mm(t, v1=v1):
            t.wait_ge(vS.h, v1)
            t.matmul(self.ps1[:, 65:66], self.ones_f[0:1, :], self.wsp[0:1, 2:3],
                     start=True, stop=True)
            t.matmul(self.ps1[:, 66:67], self.ones_f[0:1, :], self.wsp[0:1, 1:2],
                     start=True, stop=True).then_inc(tS.h, 1)
        self.te(winv_mm)
        g1 = vS.bump()

        def winv_ev(v, t2b=t2b):
            v.wait_ge(tS.h, t2b)
            v.tensor_copy(self.wsp[:, 3:4], self.ps1[:, 65:66])
            v.tensor_copy(self.wsp[:, 5:6], self.ps1[:, 66:67]).then_inc(vS.h, 1)
        self.ve(winv_ev)

        q_bufs = [self.gv(16384, 2048, BF16), self.gv(20480, 2048, BF16)]
        t_buf = self.gv(24576, 2048, F32)
        f_buf = self.gv(32768, 1024, F32)
        st_done = {}
        for i in range(NKD):
            buf = bufs[i % 2]
            qb = q_bufs[i % 2]
            wv = st_done.get(i - 2)
            qt = qS.bump(16)

            def ld(sync, i=i, buf=buf, wv=wv, g1=g1):
                if i == 0:
                    sync.wait_ge(vS.h, g1)
                if wv:
                    sync.wait_ge(qS.h, wv)
                sync.dma_start(out=buf,
                               in_=self.wq_h.ap()[i * 128:(i + 1) * 128, :]).then_inc(qS.h, 16)
            self.sy(ld)
            vt = vS.bump()

            def quant(v, buf=buf, qb=qb, qt=qt):
                v.wait_ge(qS.h, qt)
                for hb in range(2):
                    sl = slice(hb * 1024, (hb + 1) * 1024)
                    tb = t_buf[:, sl]
                    v.scalar_tensor_tensor(tb, buf[:, sl], self.wsp[:, 3:4], buf[:, sl],
                                           Alu.mult, Alu.bypass)
                    v.tensor_scalar_add(f_buf, tb, 12582912.0)
                    v.tensor_scalar_add(tb, f_buf, -12582912.0)
                v.tensor_scalar(qb, t_buf, 1.0, -1.0, Alu.min, Alu.max).then_inc(vS.h, 1)
            self.ve(quant)
            st = qS.bump(16)

            def stf(sync, i=i, qb=qb, vt=vt):
                sync.wait_ge(vS.h, vt)
                sync.dma_start(out=self.sp_wq.ap()[i * 128:(i + 1) * 128, :],
                               in_=qb).then_inc(qS.h, 16)
            self.sy(stf)
            st_done[i] = st

    def p1_layernorm(self):
        S, SH = self.S, self.SH
        qS, vS, tS, sS = self.qS, self.vS, self.tS, self.sS
        xld = [self.gv(0, SH, F32), self.gv(4 * SH, SH, F32)]
        ve_done = {}
        for k in range(NKD):
            buf = xld[k % 2]
            wv = ve_done.get(k - 2)
            qt = qS.bump(16)

            def ld(sync, k=k, buf=buf, wv=wv):
                if wv:
                    sync.wait_ge(vS.h, wv)
                sync.dma_start(out=buf,
                               in_=self.x_h.ap()[k * 128:(k + 1) * 128, :]).then_inc(qS.h, 16)
            self.sy(ld)
            vt = vS.bump()

            def park(v, k=k, buf=buf, qt=qt):
                v.wait_ge(qS.h, qt)
                v.tensor_copy(self.xnt_h[:, k * 128:(k + 1) * 128], buf[:, 0:128])
                v.tensor_copy(self.fsh[:, k * S:(k + 1) * S], buf[:, 128:SH]).then_inc(vS.h, 1)
            self.ve(park)
            ve_done[k] = vt

        pall = vS.n
        t1 = tS.bump()

        def sums(t, pall=pall):
            t.wait_ge(vS.h, pall)
            for k in range(NKD):
                for (o, w) in self.SSL:
                    ins = t.matmul(self.ps0[0:1, o:o + w], self.ones_bf[:, 0:1],
                                   self.fsh[:, k * S + o: k * S + o + w],
                                   start=(k == 0), stop=(k == NKD - 1))
                ins = t.matmul(self.ps1[0:1, 0:128], self.ones_bf[:, 0:1],
                               self.xnt_h[:, k * 128:(k + 1) * 128],
                               start=(k == 0), stop=(k == NKD - 1))
            ins.then_inc(tS.h, 1)
        self.te(sums)
        v1 = vS.bump()

        def sum_evac(v, t1=t1):
            v.wait_ge(tS.h, t1)
            v.tensor_copy(self.rows[0:1, 128:SH], self.ps0[0:1, 0:S])
            v.tensor_copy(self.rows[0:1, 0:128], self.ps1[0:1, 0:128]).then_inc(vS.h, 1)
        self.ve(sum_evac)
        s1 = sS.bump()

        def squares(s, v1=v1):
            s.wait_ge(vS.h, v1)
            for k in range(NKD):
                s.activation(self.fsh[:, k * S:(k + 1) * S],
                             self.fsh[:, k * S:(k + 1) * S], Act.Square)
            s.activation(self.xnt_h[:, :], self.xnt_h[:, :], Act.Square).then_inc(sS.h, 1)
        self.se(squares)
        t2 = tS.bump()

        def sqsums(t, s1=s1):
            t.wait_ge(sS.h, s1)
            for k in range(NKD):
                for (o, w) in self.SSL:
                    ins = t.matmul(self.ps0[0:1, o:o + w], self.ones_bf[:, 0:1],
                                   self.fsh[:, k * S + o: k * S + o + w],
                                   start=(k == 0), stop=(k == NKD - 1))
                ins = t.matmul(self.ps1[0:1, 0:128], self.ones_bf[:, 0:1],
                               self.xnt_h[:, k * 128:(k + 1) * 128],
                               start=(k == 0), stop=(k == NKD - 1))
            ins.then_inc(tS.h, 1)
        self.te(sqsums)
        v2 = vS.bump()

        def fin_rows(v, t2=t2):
            v.wait_ge(tS.h, t2)
            v.tensor_copy(self.rows[32:33, 128:SH], self.ps0[0:1, 0:S])
            v.tensor_copy(self.rows[32:33, 0:128], self.ps1[0:1, 0:128]).then_inc(vS.h, 1)
        self.ve(fin_rows)
        m_sb = self.fvw(0, SH, F32)
        msq = self.fvw(4 * SH, SH, F32)
        rstd_f = self.fvw(8 * SH, SH, F32)
        r2_f = self.fvw(12 * SH, SH, F32)
        v2b = vS.n
        t3 = tS.bump()

        def bc_m(t, v2b=v2b):
            t.wait_ge(vS.h, v2b)
            for (o, w) in self.SSL:
                t.matmul(self.ps0[:, o:o + w], self.ones_f[0:1, :],
                         self.rows[0:1, 128 + o:128 + o + w], start=True, stop=True)
            t.matmul(self.ps1[:, 0:128], self.ones_f[0:1, :],
                     self.rows[0:1, 0:128], start=True, stop=True).then_inc(tS.h, 1)
        self.te(bc_m)
        v2c = vS.bump()

        def ev_m(v, t3=t3):
            v.wait_ge(tS.h, t3)
            v.tensor_scalar_mul(m_sb[:, 128:SH], self.ps0[:, 0:S], 1.0 / DM)
            v.tensor_scalar_mul(m_sb[:, 0:128], self.ps1[:, 0:128], 1.0 / DM)
            v.drain()
            v.tensor_tensor(msq, m_sb, m_sb, Alu.mult).then_inc(vS.h, 1)
        self.ve(ev_m)
        t4 = tS.bump()

        def bc_q(t, v2c=v2c):
            t.wait_ge(vS.h, v2c)
            for (o, w) in self.SSL:
                t.matmul(self.ps0[:, o:o + w], self.ones_f[32:33, :],
                         self.rows[32:33, 128 + o:128 + o + w], start=True, stop=True)
            t.matmul(self.ps1[:, 0:128], self.ones_f[32:33, :],
                     self.rows[32:33, 0:128], start=True, stop=True).then_inc(tS.h, 1)
        self.te(bc_q)
        v2d = vS.bump()

        def var_f(v, t4=t4):
            v.wait_ge(tS.h, t4)
            v.scalar_tensor_tensor(rstd_f[:, 128:SH], self.ps0[:, 0:S], 1.0 / DM,
                                   msq[:, 128:SH], Alu.mult, Alu.subtract)
            v.scalar_tensor_tensor(rstd_f[:, 0:128], self.ps1[:, 0:128], 1.0 / DM,
                                   msq[:, 0:128], Alu.mult, Alu.subtract)
            v.drain()
            v.tensor_scalar_add(rstd_f, rstd_f, EPS).then_inc(vS.h, 1)
        self.ve(var_f)
        s2 = sS.bump()

        def rstd_act(s, v2d=v2d):
            s.wait_ge(vS.h, v2d)
            s.activation(rstd_f, rstd_f, Act.Sqrt).then_inc(sS.h, 1)
        self.se(rstd_act)
        v3 = vS.bump()

        def hilo_rows(v, s2=s2):
            v.wait_ge(sS.h, s2)
            v.reciprocal(rstd_f, rstd_f)
            v.scalar_tensor_tensor(r2_f, m_sb, -1.0, rstd_f, Alu.mult, Alu.mult)
            v.memset(self.fvw(16 * SH + 4 * S, S, F32), 0.0)
            v.memset(self.fvw(16 * SH, S, F32), 0.0).then_inc(vS.h, 1)
        self.ve(hilo_rows)

        amax = self.fvw(16 * SH + 4 * S, S, F32)
        amin = self.fvw(16 * SH, S, F32)
        xn_f = self.gv(8 * SH + 4 * S, SH, F32)
        half = SH // 2

        def _slices(hw):
            out, o = [], 0
            while o < hw:
                w = min(512, hw - o)
                out.append((o, w))
                o += w
            return out

        for hf in range(2):
            base = hf * half
            for k in range(NKD):
                buf = xld[k % 2]
                wv = vS.n if k >= 1 else None
                qt = qS.bump(16)

                def ld(sync, k=k, buf=buf, base=base, wv=wv):
                    if wv and k >= 2:
                        sync.wait_ge(vS.h, wv)
                    sync.dma_start(out=buf[:, 0:half],
                                   in_=self.x_h.ap()[k * 128:(k + 1) * 128,
                                                     base:base + half]).then_inc(qS.h, 16)
                self.sy(ld)
                vt = vS.bump()

                def xn_ops(v, k=k, buf=buf, hf=hf, base=base, qt=qt):
                    v.wait_ge(qS.h, qt)
                    v.tensor_tensor(xn_f[:, 0:half], buf[:, 0:half],
                                    rstd_f[:, base:base + half], Alu.mult)
                    v.tensor_tensor(xn_f[:, 0:half], xn_f[:, 0:half],
                                    r2_f[:, base:base + half], Alu.add)
                    if hf == 0:
                        v.tensor_copy(self.xnt_h[:, k * 128:(k + 1) * 128], xn_f[:, 0:128])
                        v.tensor_copy(self.xnt[:, k * S: k * S + (half - 128)], xn_f[:, 128:half])
                        v.tensor_tensor(amax[:, 0:half - 128], amax[:, 0:half - 128],
                                        xn_f[:, 128:half], Alu.max)
                        v.tensor_tensor(amin[:, 0:half - 128], amin[:, 0:half - 128],
                                        xn_f[:, 128:half], Alu.min).then_inc(vS.h, 1)
                    else:
                        mo = half - 128
                        v.tensor_copy(self.xnt[:, k * S + mo: k * S + mo + half],
                                      xn_f[:, 0:half])
                        v.tensor_tensor(amax[:, mo:mo + half], amax[:, mo:mo + half],
                                        xn_f[:, 0:half], Alu.max)
                        v.tensor_tensor(amin[:, mo:mo + half], amin[:, mo:mo + half],
                                        xn_f[:, 0:half], Alu.min).then_inc(vS.h, 1)
                self.ve(xn_ops)
                st = qS.bump(16)

                def spill(sync, k=k, hf=hf, vt=vt):
                    sync.wait_ge(vS.h, vt)
                    if hf == 0:
                        ap = self.sp_xn.ap()[k, :, 0:half - 128]
                        src = xn_f[:, 128:half]
                    else:
                        ap = self.sp_xn.ap()[k, :, half - 128: 2 * half - 128]
                        src = xn_f[:, 0:half]
                    sync.dma_start(out=ap, in_=src).then_inc(qS.h, 16)
                self.sy(spill)
                g1 = vS.bump()

                def guard(v, st=st):
                    v.wait_ge(qS.h, st)
                    v.engine_nop().then_inc(vS.h, 1)
                self.ve(guard)

        self._amax = amax
        self._amin = amin

    def p1_ascale(self):
        S, SH, NCH = self.S, self.SH, self.NCH
        qS, vS, tS, sS = self.qS, self.vS, self.tS, self.sS
        va = vS.bump()

        def as_comb(v):
            v.tensor_scalar_mul(self._amin, self._amin, -1.0)
            v.tensor_tensor(self._amax, self._amax, self._amin,
                            Alu.max).then_inc(vS.h, 1)
        self.ve(as_comb)
        NCH = self.NCH
        ta = tS.bump()

        def as_tp(t, va=va):
            t.wait_ge(vS.h, va)
            for c in range(NCH):
                ins = t.transpose(self.ps1[:, c * 128:(c + 1) * 128],
                                  self._amax[:, c * 128:(c + 1) * 128], self.c_id[:, :])
            ins.then_inc(tS.h, 1)
        self.te(as_tp)
        vb = vS.bump()

        def as_red(v, ta=ta):
            v.wait_ge(tS.h, ta)
            for c in range(NCH):
                ins = v.tensor_reduce(self.sml[:, 48 + c:49 + c],
                                      self.ps1[:, c * 128:(c + 1) * 128],
                                      AxX, Alu.max)
            ins.then_inc(vS.h, 1)
        self.ve(as_red)
        qa1 = qS.bump(16 * NCH + 16)

        def as_bounce(sync, vb=vb):
            sync.wait_ge(vS.h, vb)
            for c in range(NCH):
                sync.dma_start(out=self.sp_as.ap()[c * 128:(c + 1) * 128].unsqueeze(1),
                               in_=self.sml[:, 48 + c:49 + c]).then_inc(qS.h, 16)
            sync.dma_start(out=self.rows[64:65, 128:SH],
                           in_=self.sp_as.ap().unsqueeze(0)).then_inc(qS.h, 16)
        self.sy(as_bounce)
        vc = vS.bump()

        def as_fin(v, qa1=qa1):
            v.wait_ge(qS.h, qa1)
            v.tensor_scalar_max(self.rows[64:65, 128:SH], self.rows[64:65, 128:SH],
                                1e-8).then_inc(vS.h, 1)
        self.ve(as_fin)

        if self.debug_dump:
            qd = qS.bump(16 * NKD)

            def dump(sync, va=va):
                sync.wait_ge(vS.h, va)
                for k in range(NKD):
                    sync.dma_start(out=self.dbg["dbg_xn"].ap()[k],
                                   in_=self.xnt[:, k * S:(k + 1) * S]).then_inc(qS.h, 16)
            self.sy(dump)


    def p2a_inproj_conv(self):
        S, SH = self.S, self.SH
        qS, qG, vS, tS, sS = self.qS, self.qG, self.vS, self.tS, self.sS
        wb = [self.gv(0, 2048, BF16), self.gv(4096, 2048, BF16)]
        xbc_w = self.gv(8192, SH, F32)
        xcs = [self.gv(8192 + 4 * SH, S, BF16), self.gv(8192 + 4 * SH + 2 * S, S, BF16)]
        acc_b = 8192 + 4 * SH + 4 * S
        accv = self.gv(acc_b, S, F32)
        sgv = self.gv(acc_b + 4 * S, S, BF16)
        mm_done = {}
        ev_done = {}
        st_done = {}
        for m in range(35):
            buf = wb[m % 2]
            ncols = 128 if m < 34 else 64
            coff = 4096 + 128 * m
            wprev = mm_done.get(m - 2)
            qw = qG.bump(16)

            def wld(g, buf=buf, ncols=ncols, coff=coff, wprev=wprev):
                if wprev:
                    g.wait_ge(tS.h, wprev)
                wap = AP(self.w_in_h, coff, [[DPROJ, 128], [DPROJ * 128, NKD], [1, ncols]])
                g.dma_start(out=buf[:, 0:NKD * ncols].rearrange("p (k c) -> p k c", k=NKD),
                            in_=wap).then_inc(qG.h, 16)
            self.gp(wld)
            evp = ev_done.get(m - 1)
            tm = tS.bump()

            def mms(t, buf=buf, ncols=ncols, qw=qw, evp=evp):
                t.wait_ge(qG.h, qw)
                if evp:
                    t.wait_ge(vS.h, evp)
                rows = slice(0, ncols)
                for k in range(NKD):
                    lhs = buf[:, k * ncols:(k + 1) * ncols]
                    st = (k == 0)
                    sp = (k == NKD - 1)
                    t.matmul(self.ps1[rows, 0:128], lhs,
                             self.xnt_h[:, k * 128:(k + 1) * 128], start=st, stop=sp)
                    for (o, w) in self.SSL:
                        ins = t.matmul(self.ps0[rows, o:o + w], lhs,
                                       self.xnt[:, k * S + o: k * S + o + w],
                                       start=st, stop=sp)
                ins.then_inc(tS.h, 1)
            self.te(mms)
            mm_done[m] = tm

            if m < 34:
                ob = xcs[m % 2]
                ve1 = vS.bump()

                def evac(v, tm=tm):
                    v.wait_ge(tS.h, tm)
                    v.tensor_copy(xbc_w[:, 0:128], self.ps1[:, 0:128])
                    v.tensor_copy(xbc_w[:, 128:SH], self.ps0[:, 0:S]).then_inc(vS.h, 1)
                self.ve(evac)
                ev_done[m] = ve1
                stp = st_done.get(m - 2)
                ve2 = vS.bump()

                def conv(v, m=m, stp=stp):
                    if stp:
                        v.wait_ge(qS.h, stp)
                    w4 = self.cw_sb[:, m * 4: m * 4 + 4]
                    v.tensor_scalar_mul(accv, xbc_w[:, 125:125 + S], w4[:, 0:1])
                    v.scalar_tensor_tensor(accv, xbc_w[:, 126:126 + S], w4[:, 1:2], accv,
                                           Alu.mult, Alu.add)
                    v.scalar_tensor_tensor(accv, xbc_w[:, 127:127 + S], w4[:, 2:3], accv,
                                           Alu.mult, Alu.add)
                    v.scalar_tensor_tensor(accv, xbc_w[:, 128:128 + S], w4[:, 3:4], accv,
                                           Alu.mult, Alu.add)
                    v.tensor_scalar_add(accv, accv, self.cb_sb[:, m:m + 1]).then_inc(vS.h, 1)
                self.ve(conv)
                s1 = sS.bump()

                def sig(s, ve2=ve2):
                    s.wait_ge(vS.h, ve2)
                    s.activation(sgv, accv, Act.Sigmoid).then_inc(sS.h, 1)
                self.se(sig)
                ve3 = vS.bump()

                def silu(v, ob=ob, s1=s1):
                    v.wait_ge(sS.h, s1)
                    v.tensor_tensor(ob, accv, sgv, Alu.mult).then_inc(vS.h, 1)
                self.ve(silu)
                st = qS.bump(16)

                def store(sync, m=m, ob=ob, ve3=ve3):
                    sync.wait_ge(vS.h, ve3)
                    sync.dma_start(out=self.sp_xc.ap()[m], in_=ob).then_inc(qS.h, 16)
                self.sy(store)
                st_done[m] = st
            else:
                u = self.gv(4096, S, F32)[0:64, :]
                e = self.gv(12288, S, F32)[0:64, :]
                dacs = self.gv(20480, S, F32)[0:64, :]
                dtt = self.gv(28672, S, F32)[0:64, :]
                one_t = self.gv(36864, 128, F32)[0:64, :]
                v1 = vS.bump()

                def dt_v1(v, tm=tm):
                    v.wait_ge(tS.h, tm)
                    v.tensor_scalar(u, self.ps0[0:64, 0:S], self.dtb_col[:, :], 30.0,
                                    Alu.add, Alu.min).then_inc(vS.h, 1)
                self.ve(dt_v1)
                s1 = sS.bump()

                def dt_s1(s, v1=v1):
                    s.wait_ge(vS.h, v1)
                    s.activation(e, u, Act.Exp)
                    s.activation(dtt, e, Act.Ln, bias=1.0).then_inc(sS.h, 1)
                self.se(dt_s1)
                v2 = vS.bump()

                def dt_v2(v, s1=s1):
                    s = s1
                    v.wait_ge(sS.h, s)
                    v.tensor_scalar_max(e, dtt, 1e-38).then_inc(vS.h, 1)
                self.ve(dt_v2)
                s2 = sS.bump()

                def dt_s2(s, v2=v2):
                    s.wait_ge(vS.h, v2)
                    s.activation(e, e, Act.Ln).then_inc(sS.h, 1)
                self.se(dt_s2)
                v3 = vS.bump()

                def dt_v3(v, s2=s2):
                    v.wait_ge(sS.h, s2)
                    v.tensor_scalar_mul(u, dtt, self.a_col[:, :])
                    v.memset(one_t, 1.0)
                    v.drain()
                    for j in range(self.NCH):
                        v.tensor_tensor_scan(dacs[:, j * 128:(j + 1) * 128], one_t,
                                             u[:, j * 128:(j + 1) * 128], 0.0,
                                             Alu.mult, Alu.add)
                    v.drain()
                    v.tensor_sub(e, dacs, e).then_inc(vS.h, 1)
                self.ve(dt_v3)
                qd = qS.bump(32 if not self.debug_dump else 64)

                def dt_spill(sync, v3=v3):
                    sync.wait_ge(vS.h, v3)
                    sync.dma_start(out=self.sp_da.ap()[0], in_=dacs).then_inc(qS.h, 16)
                    sync.dma_start(out=self.sp_da.ap()[1], in_=e).then_inc(qS.h, 16)
                    if self.debug_dump:
                        sync.dma_start(out=self.dbg["dbg_dt"].ap()[0:NH, :],
                                       in_=dtt).then_inc(qS.h, 16)
                        sync.dma_start(out=self.dbg["dbg_dt"].ap()[NH:2 * NH, :],
                                       in_=dacs).then_inc(qS.h, 16)
                self.sy(dt_spill)

    def p5q_quant(self):
        S, SH = self.S, self.SH
        qS, vS, tS, sS = self.qS, self.vS, self.tS, self.sS
        sbc = self.gv(12 * 2048, 2048, F32)
        vt0 = vS.n
        t1 = tS.bump()

        def asbc(t, vt0=vt0):
            t.wait_ge(vS.h, vt0)
            for (o, w) in self.SSL:
                ins = t.matmul(self.ps0[:, o:o + w], self.ones_f[64:65, :],
                               self.rows[64:65, 128 + o:128 + o + w],
                               start=True, stop=True)
            ins.then_inc(tS.h, 1)
        self.te(asbc)
        v1 = vS.bump()

        def sbuild(v, t1=t1):
            v.wait_ge(tS.h, t1)
            v.tensor_copy(sbc[:, 0:S], self.ps0[:, 0:S])
            v.reciprocal(sbc[:, 0:S], sbc[:, 0:S])
            v.tensor_scalar_mul(sbc[:, 0:S], sbc[:, 0:S], 127.0).then_inc(vS.h, 1)
        self.ve(sbuild)

        xbufs = [self.gv(0, S, F32), self.gv(4 * S, S, F32)]
        tbuf = self.gv(8 * S, S, F32)
        qdone = {}
        for k in range(NKD):
            xbuf = xbufs[k % 2]
            qt = qS.bump(16)
            pv = qdone.get(k - 2)

            def ld(sync, k=k, xbuf=xbuf, pv=pv):
                if pv:
                    sync.wait_ge(vS.h, pv)
                sync.dma_start(out=xbuf, in_=self.sp_xn.ap()[k]).then_inc(qS.h, 16)
            self.sy(ld)
            vt = vS.bump()

            def quant(v, k=k, xbuf=xbuf, qt=qt, v1=v1):
                v.wait_ge(qS.h, qt)
                v.wait_ge(vS.h, v1)
                v.tensor_tensor(tbuf, xbuf, sbc[:, 0:S], Alu.mult)
                v.tensor_scalar_add(tbuf, tbuf, 12582912.0)
                v.tensor_scalar_add(tbuf, tbuf, -12582912.0)
                v.tensor_copy(self.fsh[:, k * S:(k + 1) * S], tbuf).then_inc(vS.h, 1)
            self.ve(quant)
            qdone[k] = vt
            if self.debug_dump:
                qS.bump(16)

                def dxq(sync, k=k, vt=vt):
                    sync.wait_ge(vS.h, vt)
                    sync.dma_start(out=self.dbg["dbg_xq"].ap()[k],
                                   in_=self.fsh[:, k * S:(k + 1) * S]).then_inc(qS.h, 16)
                self.sy(dxq)
        self._p5q_marks = [qdone[k] for k in range(NKD)]

    def _chunk_loads(self, j, with_c, guards=None, c_dst=None):
        S, qS = self.S, self.qS
        xcj = self.gv(0, DIN, BF16)
        if c_dst is None:
            c_dst = self.gv(8448, 128, BF16)
        n = 5 if with_c else 4
        if guards is None:
            guards = [(self.vS.h, self.vS.n), (self.sS.h, self.sS.n),
                      (self.tS.h, self.tS.n), (self.qA.h, self.qA.n)]
        qt = qS.bump(16 * n)

        def ld(sync, j=j, with_c=with_c, guards=guards, qt=qt, n=n):
            for h, c in guards:
                if c:
                    sync.wait_ge(h, c)
            ap = AP(self.sp_xc, j * 128, [[S, 128], [128 * S, 32], [1, 128]])
            sync.dma_start(out=xcj.rearrange("p (g c) -> p g c", g=32), in_=ap).then_inc(qS.h, 16)
            bap = AP(self.sp_xc, 32 * 128 * S + j * 128, [[S, 128], [1, 128]])
            sync.dma_start(out=self.gv(8192, 128, BF16), in_=bap).then_inc(qS.h, 16)
            if with_c:
                cap = AP(self.sp_xc, 33 * 128 * S + j * 128, [[S, 128], [1, 128]])
                sync.dma_start(out=c_dst, in_=cap).then_inc(qS.h, 16)
            dap = AP(self.sp_da, j * 128, [[S, NH], [1, 128]])
            sync.dma_start(out=self.dacs_c[:, :], in_=dap).then_inc(qS.h, 16)
            cpap = AP(self.sp_da, NH * S + j * 128, [[S, NH], [1, 128]])
            sync.dma_start(out=self.cp_c[:, :], in_=cpap).then_inc(qS.h, 16)
        self.sy(ld)
        return qt

    def _chunk_common(self, j, ldtgt, tpose_waits=(), psum_free=None):
        qA, qS, vS, tS, sS = self.qA, self.qS, self.vS, self.tS, self.sS
        xcj = self.gv(0, DIN, BF16)
        b_sl = self.gv(8192, 128, BF16)
        qt = qA.bump(32)

        def tposes(sc, ldtgt=ldtgt, tpose_waits=tuple(tpose_waits)):
            sc.wait_ge(self.qS.h, ldtgt)
            for h, v in tpose_waits:
                sc.wait_ge(h, v)
            sc.dma_start(out=self.f_xtok.rearrange("p (g c) -> p g c", g=32),
                         in_=xcj.rearrange("p (g c) -> p g c", g=32),
                         transpose=True).then_inc(qA.h, 16)
            sc.dma_start(out=self.f_btok, in_=b_sl, transpose=True).then_inc(qA.h, 16)
        self.se(tposes)
        s0 = sS.bump()

        def cdexp_a(s, ldtgt=ldtgt):
            s.wait_ge(self.qS.h, ldtgt)
            s.activation(self.sml[0:64, 48:49], self.dacs_c[:, 127:128],
                         Act.Exp).then_inc(sS.h, 1)
        self.se(cdexp_a)
        tcd = tS.n
        qcd = qS.bump(32)

        def cd_bounce(sync, s0=s0):
            sync.wait_ge(sS.h, s0)
            sync.dma_start(out=self.sp_cd.ap().unsqueeze(1),
                           in_=self.sml[0:64, 48:49]).then_inc(qS.h, 16)
            sync.dma_start(out=self.rows[0:1, 0:NH],
                           in_=self.sp_cd.ap().unsqueeze(0)).then_inc(qS.h, 16)
        self.sy(cd_bounce)
        vcd = qcd
        v1n = vS.bump()

        def negE2_v(v, ldtgt=ldtgt):
            v.wait_ge(self.qS.h, ldtgt)
            v.tensor_scalar_sub(self.e2w[:, :], self.cp_c[:, :],
                                self.dacs_c[:, 127:128]).then_inc(vS.h, 1)
        self.ve(negE2_v)
        t1 = tS.bump()

        def petr(t, v1n=v1n, psum_free=psum_free):
            t.wait_ge(vS.h, v1n)
            if psum_free is not None:
                t.wait_ge(vS.h, psum_free)
            t.transpose(self.ps1[:, 1600:1664], self.e2w[:, :],
                        self.c_id[0:64, 0:64]).then_inc(tS.h, 1)
        self.te(petr)
        s1 = sS.bump()

        def acts(s, t1=t1):
            s.wait_ge(tS.h, t1)
            s.activation(self.e2_tok[:, :], self.ps1[:, 1600:1664], Act.Exp,
                         scale=-1.0).then_inc(sS.h, 1)
        self.se(acts)
        t2 = tS.bump()

        def cdmm(t, qcd=qcd):
            t.wait_ge(qS.h, qcd)
            ev = self.rows[0:1, 0:NH].rearrange("a (g r) -> a g r", r=2)
            t.matmul(self.ps1[0:64, 1664:1696], self.ones_f[0:1, 0:64],
                     ev[:, :, 0:1].squeeze(2), start=True, stop=True)
            t.matmul(self.ps1[64:128, 1664:1696], self.ones_f[0:1, 0:64],
                     ev[:, :, 1:2].squeeze(2), start=True, stop=True).then_inc(tS.h, 1)
        self.te(cdmm)
        v4 = vS.bump()

        def cdev(v, t2=t2):
            v.wait_ge(tS.h, t2)
            v.tensor_copy(self.cdexp[:, :], self.ps1[:, 1664:1696])
            v.drain()
            v.tensor_mul(self.p_run[:, :], self.p_run[:, :],
                         self.cdexp[:, :]).then_inc(vS.h, 1)
        self.ve(cdev)
        return qt, v4, s1, {"qA": qt, "sE": s0, "vN": v1n, "t2": t2}

    def _states_scan(self, j, ttgt, cdt, e2s, s_pre=None, ps_free_s=None):
        vS, tS, qA, sS = self.vS, self.tS, self.qA, self.sS
        v1 = vS.bump()

        def mkxe2(v, ttgt=ttgt, cdt=cdt, e2s=e2s):
            v.wait_ge(qA.h, ttgt)
            v.wait_ge(vS.h, cdt)
            v.wait_ge(sS.h, e2s)
            e2b = self.e2_tok[:, :].unsqueeze(2).broadcast_to([128, NH, HD])
            v.tensor_tensor(self.f_xe2.rearrange("p (h d) -> p h d", h=NH),
                            self.f_xtok.rearrange("p (h d) -> p h d", h=NH),
                            e2b, Alu.mult).then_inc(vS.h, 1)
        self.ve(mkxe2)
        tbase = self.tS.n
        tmarks = [tS.bump() for _ in range(4)]

        def smms(t, v1=v1, ps_free_s=ps_free_s):
            t.wait_ge(vS.h, v1)
            if ps_free_s is not None:
                t.wait_ge(sS.h, ps_free_s)
            for g in range(NG):
                ps = self.ps0 if g < 16 else self.ps1
                ins = t.matmul(ps[:, (g % 16) * 128:(g % 16) * 128 + 128],
                               self.f_xe2[:, g * 128:(g + 1) * 128], self.f_btok,
                               start=True, stop=True)
                if g % 8 == 7:
                    ins.then_inc(tS.h, 1)
        self.te(smms)
        v2 = vS.bump()

        def scan(v, tmarks=tmarks, s_pre=s_pre):
            if s_pre is not None:
                v.wait_ge(sS.h, s_pre)
            for g in range(NG):
                if g % 8 == 0:
                    v.wait_ge(tS.h, tmarks[g // 8])
                ps = self.ps0 if g < 16 else self.ps1
                ins = v.scalar_tensor_tensor(
                    self.s_run[:, g * 128:(g + 1) * 128],
                    self.s_run[:, g * 128:(g + 1) * 128],
                    self.cdexp[:, g:g + 1],
                    ps[:, (g % 16) * 128:(g % 16) * 128 + 128], Alu.mult, Alu.add)
            ins.then_inc(vS.h, 1)
        self.ve(scan)
        return v2, tmarks[-1], v1

    def p3a_states(self):
        prev = None
        for j in range(self.NCH):
            if prev is None:
                guards, tw, pf = None, (), None
            else:
                guards = [(self.qA.h, prev["qA"]), (self.sS.h, prev["sE"]),
                          (self.vS.h, prev["vN"])]
                tw = [(self.vS.h, prev["mv"]), (self.tS.h, prev["mk"])]
                pf = prev["scan"]
            ldtgt = self._chunk_loads(j, with_c=False, guards=guards)
            ttgt, cdt, e2s, cmk = self._chunk_common(j, ldtgt, tpose_waits=tw,
                                                     psum_free=pf)
            sst, smk, smv = self._states_scan(j, ttgt, cdt, e2s)
            prev = dict(cmk, mk=smk, mv=smv, scan=sst)
        vt = self.vS.n
        qt = self.qS.bump(32)
        self._contrib_qt = qt

        def contrib(sync, vt=vt):
            sync.wait_ge(self.vS.h, vt)
            sync.dma_start(out=self.ag_i.ap()[:, 0:DIN],
                           in_=self.s_run[:, :]).then_inc(self.qS.h, 16)
            sync.dma_start(out=self.ag_i.ap()[:, DIN:DIN + NG],
                           in_=self.p_run[:, :]).then_inc(self.qS.h, 16)
        self.sy(contrib)

    def p_ag_start(self):
        qS, cS = self.qS, self.cS
        qt = qS.n
        self._ag_c1 = cS.bump()

        def cc(g, qt=qt):
            g.wait_ge(qS.h, qt)
            g.collective_compute(
                "AllGather", Alu.bypass, replica_groups=[list(range(self.n_cores))],
                ins=[self.ag_i.ap()], outs=[self.ag_o.ap()]).then_inc(cS.h, 1)
        self.gp(cc)

    def p_ag_combine(self):
        qS, cS, vS = self.qS, self.cS, self.vS
        c1 = self._ag_c1
        sbuf = self.gv(16384, DIN, F32)
        dbuf = self.sml[:, 16:48]
        vz = vS.bump()

        def zer(v, cqt=self._contrib_qt):
            v.wait_ge(qS.h, cqt)
            v.memset(self.s_run[:, :], 0.0).then_inc(vS.h, 1)
        self.ve(zer)

        prev_v = vz
        for l in [l for l in [0, 1, 2, 4, 5, 6] if l < self.n_cores]:
            qt2 = qS.bump(32)

            def ldl(sync, l=l, c1=c1, prev_v=prev_v):
                sync.wait_ge(cS.h, c1)
                sync.wait_ge(vS.h, prev_v)
                sync.dma_start(out=sbuf, in_=self.ag_o.ap()[l * 128:(l + 1) * 128,
                                                            0:DIN]).then_inc(qS.h, 16)
                sync.dma_start(out=dbuf, in_=self.ag_o.ap()[l * 128:(l + 1) * 128,
                                                            DIN:DIN + NG]).then_inc(qS.h, 16)
            self.sy(ldl)
            vc = vS.bump()

            def comb(v, l=l, qt2=qt2):
                v.wait_ge(qS.h, qt2)
                pmc = self.pm_sb[:, 2 * l:2 * l + 1]
                pm1 = self.pm_sb[:, 2 * l + 1:2 * l + 2]
                v.scalar_tensor_tensor(dbuf, dbuf, pmc, pm1.broadcast_to([128, NG]),
                                       Alu.mult, Alu.add)
                v.tensor_scalar_mul(sbuf, sbuf, pmc)
                for g in range(NG):
                    ins = v.scalar_tensor_tensor(
                        self.s_run[:, g * 128:(g + 1) * 128],
                        self.s_run[:, g * 128:(g + 1) * 128],
                        dbuf[:, g:g + 1],
                        sbuf[:, g * 128:(g + 1) * 128], Alu.mult, Alu.add)
                ins.then_inc(vS.h, 1)
            self.ve(comb)
            prev_v = vc

    def p5m_ffn(self):
        S, SH = self.S, self.SH
        qS, vS, tS = self.qS, self.vS, self.tS
        wqb = [self.gv(0, 2048, BF16), self.gv(4096, 2048, BF16)]
        sf_sb = self.gv(8192, 2048, F32)
        stg = self.gv(16384, 2048, BF16)
        vt0 = vS.n
        t1 = tS.bump()

        def sfbc(t, vt0=vt0):
            t.wait_ge(vS.h, vt0)
            for (o, w) in self.SSL:
                ins = t.matmul(self.ps1[:, o:o + w], self.ones_f[64:65, :],
                               self.rows[64:65, 128 + o:128 + o + w],
                               start=True, stop=True)
            ins.then_inc(tS.h, 1)
        self.te(sfbc)
        v1 = vS.bump()

        def sfev(v, t1=t1):
            v.wait_ge(tS.h, t1)
            v.tensor_scalar(sf_sb[:, 0:S], self.ps1[:, 0:S], self.wsp[:, 5:6],
                            1.0 / 127.0, Alu.mult, Alu.mult).then_inc(vS.h, 1)
        self.ve(sfev)
        if self.debug_dump:
            qS.bump(16)
            vref = vS.n

            def dsf(sync, vref=vref):
                sync.wait_ge(vS.h, vref)
                sync.dma_start(out=self.dbg["dbg_sf"].ap(),
                               in_=sf_sb[:, 0:S]).then_inc(qS.h, 16)
            self.sy(dsf)

        mm_done = {}
        ev_done = {}
        st_done = {}
        for o in range(NKD):
            buf = wqb[o % 2]
            wprev = mm_done.get(o - 2)
            qw = qS.bump(16)

            def wld(sync, o=o, buf=buf, wprev=wprev):
                if wprev:
                    sync.wait_ge(tS.h, wprev)
                ap = AP(self.sp_wq, o * 128, [[DM, 128], [DM * 128, NKD], [1, 128]])
                sync.dma_start(out=buf.rearrange("p (k c) -> p k c", k=NKD),
                               in_=ap).then_inc(qS.h, 16)
            self.sy(wld)
            evp = ev_done.get(o - 1)
            tm = tS.bump()

            def mms(t, buf=buf, qw=qw, evp=evp, o=o, qm=tuple(self._p5q_marks)):
                t.wait_ge(qS.h, qw)
                if evp:
                    t.wait_ge(vS.h, evp)
                if o == 0:
                    t.wait_ge(vS.h, qm[NKD - 1])
                for k in range(NKD):
                    for (oo, w) in self.SSL:
                        ins = t.matmul(self.ps0[:, oo:oo + w],
                                       buf[:, k * 128:(k + 1) * 128],
                                       self.fsh[:, k * S + oo:k * S + oo + w],
                                       start=(k == 0), stop=(k == NKD - 1))
                ins.then_inc(tS.h, 1)
            self.te(mms)
            mm_done[o] = tm
            stp = st_done.get(o - 1)
            ve = vS.bump()

            def ev(v, tm=tm, v1=v1, stp=stp):
                v.wait_ge(tS.h, tm)
                v.wait_ge(vS.h, v1)
                if stp:
                    v.wait_ge(qS.h, stp)
                v.tensor_tensor(stg[:, 0:S], self.ps0[:, 0:S], sf_sb[:, 0:S],
                                Alu.mult).then_inc(vS.h, 1)
            self.ve(ev)
            ev_done[o] = ve
            st = qS.bump(16)

            def stf(sync, o=o, ve=ve):
                sync.wait_ge(vS.h, ve)
                sync.dma_start(out=self.sp_ffn.ap()[o], in_=stg[:, 0:S]).then_inc(qS.h, 16)
            self.sy(stf)
            st_done[o] = st

    def p3b_ssd(self):
        S, NCH = self.S, self.NCH
        qS, qA, vS, tS, sS = self.qS, self.qA, self.vS, self.tS, self.sS
        c_sl = self.gv(8448, 128, BF16)
        b_sl = self.gv(8192, 128, BF16)
        efl = self.gv(8704, 128, BF16)
        cbm = self.gv(8960, 128, BF16)
        tmp_b = self.gv(9216, 2048, BF16)
        trel = self.gv(13312, 2048, F32)
        yf = self.gv(21504, DIN, BF16)
        eflr = self.gv(13312, NH * 64, BF16)

        qi = qS.bump(16)

        def inde_ld(sync):
            sync.dma_start(out=self.f_drhs[0:64, :], in_=self.inde_h.ap()).then_inc(qS.h, 16)
        self.sy(inde_ld)

        prev_yst = None
        pv = None
        pv2 = None
        for j in range(NCH):
            csl = self.gv(8448, 128, BF16) if j % 2 == 0 else self.gv(29696, 128, BF16)
            if pv is None:
                guards = None
            else:
                tguard = max(pv["tC"], pv2["tY"]) if pv2 is not None else pv["tC"]
                guards = [(self.qA.h, pv["qA"]), (self.sS.h, pv["sE"]),
                          (self.vS.h, max(pv["vR"], pv["vN"])), (self.tS.h, tguard)]
            ldtgt = self._chunk_loads(j, with_c=True, guards=guards, c_dst=csl)

            v1 = vS.bump()

            def rbuild(v, ldtgt=ldtgt):
                v.wait_ge(self.qS.h, ldtgt)
                v.tensor_copy(self.rhl[0:64, :], self.dacs_c[:, :])
                v.tensor_copy(self.d_lhA[0:64, :], self.cp_c[:, :])
                v.drain()
                v.tensor_sub(self.scr64[:, :], self.dacs_c[:, :], self.rhl[0:64, :])
                v.drain()
                v.tensor_copy(self.rhl[64:128, :], self.scr64[:, :])
                v.tensor_sub(self.scr64[:, :], self.cp_c[:, :], self.d_lhA[0:64, :])
                v.drain()
                v.tensor_copy(self.d_lhB[:, :], self.scr64[:, :]).then_inc(vS.h, 1)
            self.ve(rbuild)
            s1 = sS.bump()

            def ef_act(s, ldtgt=ldtgt):
                s.wait_ge(self.qS.h, ldtgt)
                s.activation(efl[0:64, :], self.dacs_c[:, :], Act.Exp).then_inc(sS.h, 1)
            self.se(ef_act)
            qb = qS.bump(48)

            def bounce_w(sync, j=j, v1=v1, s1=s1):
                sync.wait_ge(vS.h, v1)
                sync.wait_ge(sS.h, s1)
                sync.dma_start(out=self.sp_rr.ap()[j, 0].rearrange("(a b) -> a b", a=64),
                               in_=self.rhl[0:64, :]).then_inc(qS.h, 16)
                sync.dma_start(out=self.sp_rr.ap()[j, 1].rearrange("(a b) -> a b", a=64),
                               in_=self.rhl[64:128, :]).then_inc(qS.h, 16)
                sync.dma_start(out=self.sp_ee.ap()[j].rearrange("(a b) -> a b", a=64),
                               in_=efl[0:64, :]).then_inc(qS.h, 16)
            self.sy(bounce_w)
            qb2 = qS.bump(32)

            def bounce_r(sync, j=j, qb=qb, pv=pv):
                sync.wait_ge(qS.h, qb)
                if pv is not None:
                    sync.wait_ge(sS.h, pv["sD"])
                    sync.wait_ge(tS.h, pv["tD"])
                sync.dma_start(out=self.f_drhs[64:66, :],
                               in_=self.sp_rr.ap()[j]).then_inc(qS.h, 16)
                eap = AP(self.sp_ee, j * NH * 128, [[128, 2], [256, NG], [1, 128]])
                sync.dma_start(out=eflr[0:2, :], in_=eap).then_inc(qS.h, 16)
            self.sy(bounce_r)

            ttgt, cdt, e2s, cmk = self._chunk_common(
                j, ldtgt,
                tpose_waits=([(tS.h, pv["tY"])] if pv is not None else ()),
                psum_free=(pv["vY"] if pv is not None else None))

            vsc = vS.bump()

            def scast(v, prev_yst=prev_yst):
                if prev_yst:
                    v.wait_ge(self.qG.h, prev_yst)
                v.tensor_copy(self.f_epat, self.s_run[:, :]).then_inc(vS.h, 1)
            self.ve(scast)
            qp = qA.bump(16)

            def ptr(sc, vsc=vsc):
                sc.wait_ge(vS.h, vsc)
                sc.dma_start(out=self.f_prevs.rearrange("p (g c) -> p g c", g=NG),
                             in_=self.f_epat.rearrange("p (g c) -> p g c", g=NG),
                             transpose=True).then_inc(qA.h, 16)
            self.se(ptr)

            t1 = tS.bump()

            def cbmm(t, ldtgt=ldtgt, pv=pv, csl=csl):
                t.wait_ge(self.qS.h, ldtgt)
                if pv is not None:
                    t.wait_ge(vS.h, pv["vY"])
                t.matmul(self.ps1[:, 1792:1920], b_sl, csl,
                         start=True, stop=True).then_inc(tS.h, 1)
            self.te(cbmm)
            vcb = vS.bump()

            def cbev(v, t1=t1):
                v.wait_ge(tS.h, t1)
                v.tensor_tensor(cbm, self.ps1[:, 1792:1920], self.c_trl[:, :],
                                Alu.mult).then_inc(vS.h, 1)
            self.ve(cbev)

            for r in range(2):
                pvv = vS.n
                pvs = sS.n
                te = tS.bump()

                def epmm(t, r=r, qb2=qb2, qp=qp, pvv=pvv, pvs=pvs):
                    t.wait_ge(qS.h, qb2)
                    t.wait_ge(qA.h, qp)
                    t.wait_ge(vS.h, pvv)
                    for s in range(4):
                        ins = t.matmul(self.ps0[:, s * 512:(s + 1) * 512], self.sel_bf[:, :],
                                       eflr[0:2, r * 2048 + s * 512: r * 2048 + (s + 1) * 512],
                                       start=True, stop=True)
                    ins.then_inc(tS.h, 1)
                self.te(epmm)
                vev = vS.bump()

                def epev(v, r=r, te=te):
                    v.wait_ge(tS.h, te)
                    v.tensor_copy(self.f_epat[:, r * 2048:(r + 1) * 2048],
                                  self.ps0[:, :]).then_inc(vS.h, 1)
                self.ve(epev)

            if self.debug_dump and j == 0:
                qS.bump(32)

                def dep(sync, vcb=vcb, vev=vev):
                    sync.wait_ge(vS.h, vcb)
                    sync.wait_ge(vS.h, vev)
                    sync.dma_start(out=self.dbg["dbg_cbm"].ap(),
                                   in_=cbm).then_inc(qS.h, 16)
                    sync.dma_start(out=self.dbg["dbg_ep"].ap(),
                                   in_=self.f_epat[:, :]).then_inc(qS.h, 16)
                self.sy(dep)

            sst, _smk, _smv = self._states_scan(j, ttgt, cdt, e2s)

            for hf in range(2):
                go = hf * 16
                gdone = []
                for qq in range(2):
                    q = 2 * hf + qq
                    ps = self.ps0 if qq == 0 else self.ps1
                    pv_v = vS.n
                    pv_s = sS.n
                    td = tS.bump()

                    def dmm(t, q=q, ps=ps, pv_v=pv_v, pv_s=pv_s, sst=sst):
                        t.wait_ge(vS.h, max(pv_v, sst))
                        t.wait_ge(sS.h, pv_s)
                        for s in range(4):
                            c0 = q * 2048 + s * 512
                            t.matmul(ps[:, s * 512:(s + 1) * 512], self.d_lhA[:, :],
                                     self.f_drhs[:, c0:c0 + 512], start=True, stop=False)
                            ins = t.matmul(ps[:, s * 512:(s + 1) * 512], self.d_lhB[:, :],
                                           self.f_drhs[0:64, c0:c0 + 512],
                                           start=False, stop=True)
                        ins.then_inc(tS.h, 1)
                    self.te(dmm)
                    sd = sS.bump()

                    def dexp(s, qq=qq, ps=ps, td=td):
                        s.wait_ge(tS.h, td)
                        s.activation(trel, ps[:, :], Act.Relu, bias=30.0, scale=-1.0)
                        s.activation(self.f_gt[:, qq * 2048:(qq + 1) * 2048], trel,
                                     Act.Exp, bias=30.0, scale=-1.0).then_inc(sS.h, 1)
                    self.se(dexp)
                    vg = vS.bump()

                    def gmul(v, qq=qq, sd=sd, vcb=vcb):
                        v.wait_ge(sS.h, sd)
                        v.wait_ge(vS.h, vcb)
                        cb_b = cbm.unsqueeze(1).broadcast_to([128, 16, 128])
                        v.tensor_tensor(
                            self.f_gt[:, qq * 2048:(qq + 1) * 2048].rearrange(
                                "p (h c) -> p h c", h=16), cb_b,
                            self.f_gt[:, qq * 2048:(qq + 1) * 2048].rearrange(
                                "p (h c) -> p h c", h=16), Alu.mult).then_inc(vS.h, 1)
                    self.ve(gmul)
                    gdone.append(vg)

                if self.debug_dump and j == 0:
                    qS.bump(16)

                    def dgt(sync, hf=hf, gd=gdone[-1]):
                        sync.wait_ge(vS.h, gd)
                        sync.dma_start(out=self.dbg["dbg_gt"].ap()[hf],
                                       in_=self.f_gt[:, :]).then_inc(qS.h, 16)
                    self.sy(dgt)
                ty = tS.bump()

                def ymms(t, hf=hf, go=go, gd=gdone[-1], qp=qp, sst=sst, csl=csl):
                    t.wait_ge(vS.h, max(gd, sst))
                    t.wait_ge(qA.h, qp)
                    for gi in range(16):
                        g = go + gi
                        t.matmul(self.ps1[:, gi * 128:(gi + 1) * 128],
                                 self.f_prevs[:, g * 128:(g + 1) * 128],
                                 csl, start=True, stop=True)
                    for gi in range(16):
                        for r in range(2):
                            h = 2 * (go + gi) + r
                            hl = h - 32 * hf
                            ins = t.matmul(
                                self.ps0[r * 64:(r + 1) * 64, gi * 128:(gi + 1) * 128],
                                self.f_xtok[:, h * 64:(h + 1) * 64],
                                self.f_gt[:, hl * 128:(hl + 1) * 128],
                                start=True, stop=True)
                    ins.then_inc(tS.h, 1)
                self.te(ymms)
                vy = vS.bump()

                def yev(v, hf=hf, ty=ty):
                    v.wait_ge(tS.h, ty)
                    v.tensor_tensor(tmp_b, self.ps1[:, :],
                                    self.f_epat[:, hf * 2048:(hf + 1) * 2048], Alu.mult)
                    v.tensor_tensor(yf[:, hf * 2048:(hf + 1) * 2048], tmp_b,
                                    self.ps0[:, :], Alu.add).then_inc(vS.h, 1)
                self.ve(yev)

            yvt = self.vS.n
            qy = self.qG.bump(16)

            def yst(g, j=j, yvt=yvt):
                g.wait_ge(vS.h, yvt)
                g.dma_start(out=self.sp_y.ap()[j], in_=yf).then_inc(self.qG.h, 16)
            self.gp(yst)
            prev_yst = qy
            pv2 = pv
            pv = {"qA": ttgt, "sE": cmk["sE"], "vN": cmk["vN"], "vR": v1, "tC": t1,
                  "sD": sS.n, "tD": tS.n, "tY": ty, "vY": vy}

    def p2b_gate(self):
        S, SH = self.S, self.SH
        qS, qG, vS, tS, sS = self.qS, self.qG, self.vS, self.tS, self.sS
        wb = [self.gv(0, 2048, BF16), self.gv(4096, 2048, BF16)]
        xcb = self.gv(8192, S, BF16)
        ytl = self.gv(8192 + 2 * S, S, BF16)
        zf = self.gv(8192 + 4 * S, S, F32)
        sg = self.gv(8192 + 8 * S, S, BF16)
        tb = self.gv(8192 + 10 * S, S, BF16)
        sq = self.gv(8192 + 12 * S, S, BF16)
        mm_done = {}
        gate_done = {}
        zev_done = {}
        yn_store = {}
        rms_q = []
        rms_tr = {}
        for m in range(NKI):
            buf = wb[m % 2]
            wprev = mm_done.get(m - 2)
            qw = qG.bump(16)

            def wld(g, m=m, buf=buf, wprev=wprev):
                if wprev:
                    g.wait_ge(tS.h, wprev)
                wap = AP(self.w_in_h, 128 * m, [[DPROJ, 128], [DPROJ * 128, NKD], [1, 128]])
                g.dma_start(out=buf.rearrange("p (k c) -> p k c", k=NKD),
                            in_=wap).then_inc(qG.h, 16)
            self.gp(wld)
            gprev = gate_done.get(m - 1)
            zprev = zev_done.get(m - 1)
            qt = qS.bump(32)

            def tld(sync, m=m, gprev=gprev):
                if gprev:
                    sync.wait_ge(vS.h, gprev)
                yap = AP(self.sp_y, m * 128, [[DIN, 128], [128 * DIN, self.NCH], [1, 128]])
                sync.dma_start(out=ytl[:, 0:self.NCH * 128].rearrange(
                    "p (j c) -> p j c", j=self.NCH), in_=yap).then_inc(qS.h, 16)
                sync.dma_start(out=xcb, in_=self.sp_xc.ap()[m]).then_inc(qS.h, 16)
            self.sy(tld)
            evp = gate_done.get(m - 1)
            tm = tS.bump()

            def mms(t, buf=buf, qw=qw, zprev=zprev):
                t.wait_ge(qG.h, qw)
                if zprev:
                    t.wait_ge(vS.h, zprev)
                for k in range(NKD):
                    for (o, w) in self.SSL:
                        ins = t.matmul(self.ps0[:, o:o + w],
                                       buf[:, k * 128:(k + 1) * 128],
                                       self.xnt[:, k * S + o: k * S + o + w],
                                       start=(k == 0), stop=(k == NKD - 1))
                ins.then_inc(tS.h, 1)
            self.te(mms)
            mm_done[m] = tm
            if m > 0:
                ps2, pm = rms_q.pop(0)
                tr = tS.bump()

                def rmsmm(t, m=pm, s2=ps2):
                    t.wait_ge(sS.h, s2)
                    for (o, w) in self.SSL:
                        ins = t.matmul(self.ps1[0:1, o:o + w], self.ones_bf[:, 0:1],
                                       sq[:, o:o + w],
                                       start=(m == 0), stop=(m == NKI - 1))
                    ins.then_inc(tS.h, 1)
                self.te(rmsmm)
                rms_tr[pm] = tr
            vz = vS.bump()

            def zev(v, tm=tm):
                v.wait_ge(tS.h, tm)
                v.tensor_copy(zf, self.ps0[:, 0:S]).then_inc(vS.h, 1)
            self.ve(zev)
            zev_done[m] = vz
            s1 = sS.bump()

            def sig(s, vz=vz):
                s.wait_ge(vS.h, vz)
                s.activation(sg, zf, Act.Sigmoid).then_inc(sS.h, 1)
            self.se(sig)
            ynp = yn_store.get(m - 1)
            vg = vS.bump()

            def gate(v, m=m, s1=s1, qt=qt, ynp=ynp):
                v.wait_ge(sS.h, s1)
                v.wait_ge(qS.h, qt)
                if ynp:
                    v.wait_ge(qS.h, ynp)
                v.tensor_tensor(zf, zf, sg, Alu.mult)
                v.scalar_tensor_tensor(tb, xcb, self.d_hp[:, m:m + 1], ytl, Alu.mult, Alu.add)
                v.tensor_tensor(tb, tb, zf, Alu.mult).then_inc(vS.h, 1)
            self.ve(gate)
            gate_done[m] = vg
            s2 = sS.bump()
            trp = rms_tr.get(m - 1)

            def sqs(s, vg=vg, trp=trp):
                s.wait_ge(vS.h, vg)
                if trp:
                    s.wait_ge(tS.h, trp)
                s.activation(sq, tb, Act.Square).then_inc(sS.h, 1)
            self.se(sqs)
            rms_q.append((s2, m))
            vy = vS.bump()

            def ynrm(v, m=m, s2=s2):
                v.wait_ge(sS.h, s2)
                v.tensor_scalar_mul(tb, tb, self.nw_sb[:, m:m + 1]).then_inc(vS.h, 1)
            self.ve(ynrm)
            qy = qS.bump(16)

            def ynst(sync, m=m, vy=vy):
                sync.wait_ge(vS.h, vy)
                sync.dma_start(out=self.sp_yn.ap()[m], in_=tb).then_inc(qS.h, 16)
            self.sy(ynst)
            yn_store[m] = qy

        ps2, pm = rms_q.pop(0)
        tS.bump()

        def rmslast(t, m=pm, s2=ps2):
            t.wait_ge(sS.h, s2)
            for (o, w) in self.SSL:
                ins = t.matmul(self.ps1[0:1, o:o + w], self.ones_bf[:, 0:1],
                               sq[:, o:o + w],
                               start=(m == 0), stop=(m == NKI - 1))
            ins.then_inc(tS.h, 1)
        self.te(rmslast)
        rt = self.tS.n
        vr = vS.bump()

        def rmsfin(v, rt=rt):
            v.wait_ge(tS.h, rt)
            v.tensor_copy(self.rows[32:33, 128:SH], self.ps1[0:1, 0:S]).then_inc(vS.h, 1)
        self.ve(rmsfin)

    def p4_outproj(self):
        S, SH = self.S, self.SH
        qS, qG, vS, tS = self.qS, self.qG, self.vS, self.tS
        TH = S // 2
        nsl = max(TH // 512, 1)
        ynh = self.fvw(0, NKI * TH)
        wb4 = [self.gv(0, 4096, BF16), self.gv(8192, 4096, BF16)]
        ffbs = [self.gv(16384, TH, BF16), self.p4f2.ap()[:, 0:TH]]
        xtbs = [self.gv(20480, TH, F32), self.p4x2.ap()[:, 0:TH]]
        a1 = self.gv(24576, 1024, F32)
        rstd_sb = self.gv(28672, 2048, F32)
        vt = self.vS.n
        t1 = tS.bump()

        def rbc(t, vt=vt):
            t.wait_ge(vS.h, vt)
            for (o, w) in self.SSL:
                ins = t.matmul(self.ps1[:, o:o + w], self.ones_f[32:33, :],
                               self.rows[32:33, 128 + o:128 + o + w],
                               start=True, stop=True)
            ins.then_inc(tS.h, 1)
        self.te(rbc)
        v1a = vS.bump()

        def rev(v, t1=t1):
            v.wait_ge(tS.h, t1)
            v.tensor_scalar(rstd_sb[:, 0:S], self.ps1[:, 0:S], 1.0 / DIN, EPS,
                            Alu.mult, Alu.add).then_inc(vS.h, 1)
        self.ve(rev)
        s1a = self.sS.bump()

        def rsq(s, v1a=v1a):
            s.wait_ge(vS.h, v1a)
            s.activation(rstd_sb[:, 0:S], rstd_sb[:, 0:S], Act.Sqrt).then_inc(self.sS.h, 1)
        self.se(rsq)
        v1 = vS.bump()

        def rrec(v, s1a=s1a):
            v.wait_ge(self.sS.h, s1a)
            v.reciprocal(rstd_sb[:, 0:S], rstd_sb[:, 0:S]).then_inc(vS.h, 1)
        self.ve(rrec)

        mm_done = {}
        ev_done = {}
        ost_done = {}
        for hf in range(2):
            tok0 = hf * TH
            ylds = []
            for k2 in range(NKI):
                wv = ev_done.get((hf - 1, NKD - 1)) if (hf == 1 and k2 <= 1) else None
                ctr = self.qA if k2 % 2 else qS
                qy = ctr.bump(16)

                def yld(eng, k2=k2, tok0=tok0, wv=wv, ctr=ctr):
                    if wv:
                        eng.wait_ge(vS.h, wv)
                    eng.dma_start(
                        out=ynh.rearrange("p (a c) -> p a c", a=NKI)[:, k2, :],
                        in_=self.sp_yn.ap()[k2, :, tok0:tok0 + TH]).then_inc(ctr.h, 16)
                (self.se if k2 % 2 else self.sy)(yld)
                ylds.append((ctr, qy))

            for o in range(NKD):
                buf = wb4[o % 2]
                key = (hf, o)
                pkey = (hf, o - 2) if o >= 2 else ((hf - 1, NKD + o - 2) if hf == 1 else None)
                wprev = mm_done.get(pkey) if pkey else None
                qw = qG.bump(16)

                def wld(g, o=o, buf=buf, wprev=wprev):
                    if wprev:
                        g.wait_ge(tS.h, wprev)
                    wap = AP(self.w_out_h, o * 128, [[DM, 128], [DM * 128, NKI], [1, 128]])
                    g.dma_start(out=buf.rearrange("p (k c) -> p k c", k=NKI),
                                in_=wap).then_inc(qG.h, 16)
                self.gp(wld)
                fkey = (hf, o - 2) if o >= 2 else ((hf - 1, NKD + o - 2) if hf == 1 else None)
                fevp = ev_done.get(fkey) if fkey else None
                evp = ev_done.get((hf, o - 1))
                ffb = ffbs[o % 2]
                xtb = xtbs[o % 2]
                qf = qS.bump(32)

                def fld(sync, o=o, tok0=tok0, fevp=fevp, ffb=ffb, xtb=xtb):
                    if fevp:
                        sync.wait_ge(vS.h, fevp)
                    sync.dma_start(out=ffb[:, 0:TH],
                                   in_=self.sp_ffn.ap()[o, :, tok0:tok0 + TH]).then_inc(qS.h, 16)
                    sync.dma_start(out=xtb[:, 0:TH],
                                   in_=self.x_h.ap()[o * 128:(o + 1) * 128,
                                                     128 + tok0:128 + tok0 + TH]).then_inc(
                        qS.h, 16)
                self.sy(fld)
                tm = tS.bump()

                def mms(t, buf=buf, qw=qw, ylds=tuple(ylds), v1=v1, evp=evp, o=o):
                    t.wait_ge(qG.h, qw)
                    t.wait_ge(vS.h, v1)
                    if evp:
                        t.wait_ge(vS.h, evp)
                    if o == 0:
                        for ctr, qy in ylds:
                            t.wait_ge(ctr.h, qy)
                    for k in range(NKI):
                        for s in range(nsl):
                            w = min(512, TH - s * 512)
                            ins = t.matmul(self.ps0[:, s * 512:s * 512 + w],
                                           buf[:, k * 128:(k + 1) * 128],
                                           ynh.rearrange("p (a c) -> p a c", a=NKI)[
                                               :, k, s * 512:s * 512 + w],
                                           start=(k == 0), stop=(k == NKI - 1))
                    ins.then_inc(tS.h, 1)
                self.te(mms)
                mm_done[key] = tm
                osp = ost_done.get((hf, o - 1)) or (ost_done.get((0, NKD - 1))
                                                    if hf == 1 and o == 0 else None)
                ve = vS.bump()

                def evo(v, o=o, tok0=tok0, tm=tm, qf=qf, osp=osp, ffb=ffb, xtb=xtb):
                    v.wait_ge(tS.h, tm)
                    v.wait_ge(qS.h, qf)
                    if osp:
                        v.wait_ge(qS.h, osp)
                    v.tensor_tensor(a1[:, 0:TH], self.ps0[:, 0:TH],
                                    rstd_sb[:, tok0:tok0 + TH], Alu.mult)
                    v.tensor_tensor(a1[:, 0:TH], a1[:, 0:TH], ffb[:, 0:TH], Alu.add)
                    v.tensor_tensor(a1[:, 0:TH], a1[:, 0:TH], xtb[:, 0:TH],
                                    Alu.add).then_inc(vS.h, 1)
                self.ve(evo)
                ev_done[key] = ve
                qo = qS.bump(16)

                def ost(sync, o=o, tok0=tok0, ve=ve):
                    sync.wait_ge(vS.h, ve)
                    sync.dma_start(out=self.out_h.ap()[o * 128:(o + 1) * 128,
                                                       tok0:tok0 + TH],
                                   in_=a1[:, 0:TH]).then_inc(qS.h, 16)
                self.sy(ost)
                ost_done[key] = qo

        if self.debug_dump:
            qd = qS.bump(16 * 12)

            def dcp(sync):
                sync.dma_start(out=self.dbg["dbg_xc"].ap(), in_=self.sp_xc.ap()).then_inc(qS.h, 16)
                sync.dma_start(out=self.dbg["dbg_y"].ap(), in_=self.sp_y.ap()).then_inc(qS.h, 16)
                sync.dma_start(out=self.dbg["dbg_yn"].ap(), in_=self.sp_yn.ap()).then_inc(qS.h, 16)
                sync.dma_start(out=self.dbg["dbg_ffn"].ap(), in_=self.sp_ffn.ap()).then_inc(qS.h, 16)
                sync.dma_start(out=self.dbg["dbg_sr"].ap(), in_=self.ag_i.ap()).then_inc(qS.h, 16)
                sync.dma_start(out=self.dbg["dbg_wq"].ap(), in_=self.sp_wq.ap()).then_inc(qS.h, 16)
                sync.dma_start(out=self.dbg["dbg_xnf"].ap(), in_=self.sp_xn.ap()).then_inc(qS.h, 16)
                sync.dma_start(out=self.dbg["dbg_rows"].ap(), in_=self.rows[:, :]).then_inc(qS.h, 16)
                sync.dma_start(out=self.dbg["dbg_wsp"].ap(), in_=self.wsp[:, :]).then_inc(qS.h, 16)
                sync.dma_start(out=self.dbg["dbg_da"].ap(), in_=self.sp_da.ap()).then_inc(qS.h, 16)
                sync.dma_start(out=self.dbg["dbg_ee"].ap(), in_=self.sp_ee.ap()).then_inc(qS.h, 16)
                sync.dma_start(out=self.dbg["dbg_rr"].ap(), in_=self.sp_rr.ap()).then_inc(qS.h, 16)
            self.sy(dcp)
        fq = qS.n

        def fin(sync, fq=fq):
            sync.wait_ge(qS.h, fq)
        self.sy(fin)

    def build(self):
        self.emit_all()
        nc = self.nc
        with nc.Block() as block:
            @block.sync
            def _(e):
                e = _DmaChain(e, self.qS.h)
                for k, fn in self.ops:
                    if k == "sync":
                        fn(e)

            @block.scalar
            def _(e):
                e = _DmaChain(e, self.qA.h)
                for k, fn in self.ops:
                    if k == "scalar":
                        fn(e)

            @block.vector
            def _(e):
                for k, fn in self.ops:
                    if k == "vector":
                        fn(e)

            @block.tensor
            def _(e):
                for k, fn in self.ops:
                    if k == "tensor":
                        fn(e)

            @block.gpsimd
            def _(e):
                e = _DmaChain(e, self.qG.h)
                for k, fn in self.ops:
                    if k == "gpsimd":
                        fn(e)
        return nc


def _consts():
    ident = np.eye(128, dtype=np.float32)
    tril = np.triu(np.ones((128, 128))).astype(BF)
    inde = np.zeros((64, NH * 128), dtype=BF)
    for h in range(NH):
        inde[h, h * 128:(h + 1) * 128] = -1.0
    sel = np.zeros((2, 128), dtype=BF)
    sel[0, 0:64] = 1.0
    sel[1, 64:128] = 1.0
    return ident, tril, inde, sel


def _pm_mask(core):
    seq, pos = divmod(core, 4)
    pm = np.zeros((128, 16), dtype=np.float32)
    for l in range(8):
        lseq, lpos = divmod(l, 4)
        val = 1.0 if (lseq == seq and lpos < pos) else 0.0
        pm[:, 2 * l] = val
        pm[:, 2 * l + 1] = 1.0 - val
    return pm


def make_in_maps(inputs, S):
    x = np.asarray(inputs["L0_attn_hidden_states"])
    w_in = np.ascontiguousarray(np.asarray(inputs["L0_attn_W_in"], dtype=np.float32))
    conv_w = np.ascontiguousarray(np.asarray(inputs["L0_attn_conv_w"], dtype=np.float32))
    conv_b = np.ascontiguousarray(np.asarray(inputs["L0_attn_conv_b"], dtype=np.float32))
    a_vec = np.ascontiguousarray(np.asarray(inputs["L0_attn_A"], dtype=np.float32))
    d_vec = np.ascontiguousarray(np.asarray(inputs["L0_attn_D"], dtype=np.float32))
    dtb = np.ascontiguousarray(np.asarray(inputs["L0_attn_dt_bias"], dtype=np.float32))
    nw = np.ascontiguousarray(np.asarray(inputs["L0_attn_norm_w"], dtype=np.float32))
    w_out = np.ascontiguousarray(np.asarray(inputs["L0_attn_W_out"], dtype=np.float32))
    wq = np.ascontiguousarray(np.asarray(inputs["L0_ffn_weight"], dtype=np.float32).T)
    x = np.asarray(x, dtype=np.float32)
    ident, tril, inde, sel = _consts()
    nwp = np.ascontiguousarray(nw.reshape(NKI, 128).T)
    cbp = np.ascontiguousarray(conv_b.reshape(34, 128).T)
    cwp = np.ascontiguousarray(conv_w.reshape(34, 128, 4).transpose(1, 0, 2).reshape(128, 136))
    dpk = np.ascontiguousarray(np.stack([d_vec[0::2], d_vec[1::2]]))
    in_maps = []
    per = x.shape[1] // 4
    for c in range(8):
        seq, pos = divmod(c, 4)
        t0 = pos * per
        xs = x[seq, t0:t0 + S, :]
        if pos == 0:
            halo = np.zeros((128, DM), np.float32)
        else:
            halo = x[seq, t0 - 128:t0, :]
        xf = np.ascontiguousarray(np.concatenate([halo, xs], axis=0).T)
        in_maps.append({
            "x": xf, "w_in": w_in, "conv_w": conv_w, "conv_b": conv_b,
            "a_vec": a_vec, "d_vec": d_vec, "dt_bias": dtb, "norm_w": nw,
            "w_out": w_out, "wq_t": wq,
            "c_ident": ident, "c_tril": tril, "c_inde": inde, "c_sel": sel,
            "c_pm": _pm_mask(c), "c_nwp": nwp, "c_cbp": cbp, "c_cwp": cwp, "c_dpk": dpk,
        })
    return in_maps


def kernel(**inputs):
    S = 2048
    b = Builder(S)
    nc = b.build()
    in_maps = make_in_maps(inputs, S)
    res = run_bass_kernel_spmd(nc, in_maps, core_ids=list(range(8)))
    x = np.asarray(inputs["L0_attn_hidden_states"])
    out = np.empty(x.shape, dtype=np.float32)
    per = x.shape[1] // 4
    for c in range(8):
        seq, pos = divmod(c, 4)
        out[seq, pos * per:pos * per + S, :] = np.asarray(res.results[c]["out"]).T
    return out


# version 14 (final)
# speedup vs baseline: 1.0649x; 1.0649x over previous
import numpy as np
import ml_dtypes
import concourse.bass as bass
import concourse.mybir as mybir
from concourse.bass_utils import run_bass_kernel_spmd

F32 = mybir.dt.float32
BF16 = mybir.dt.bfloat16
Alu = mybir.AluOpType
Act = mybir.ActivationFunctionType
AxX = mybir.AxisListType.X
BF = ml_dtypes.bfloat16
AP = bass.AP

DM = 2048
DIN = 4096
NH = 64
HD = 64
NG = 32
CONV = 4352
DPROJ = 8512
NKD = DM // 128
NKI = DIN // 128
EPS = 1e-5
MAGIC = 256.0


class Ctr:

    def __init__(self, nc, name):
        self.h = nc.alloc_semaphore(name)
        self.n = 0

    def bump(self, k=1):
        self.n += k
        return self.n


class _DmaChain:

    def __init__(self, e, sem):
        self._e = e
        self._sem = sem
        self._n = 0

    def dma_start(self, *a, **kw):
        if self._n:
            self._e.wait_ge(self._sem, 16 * self._n)
        self._n += 1
        return self._e.dma_start(*a, **kw)

    def __getattr__(self, k):
        return getattr(self._e, k)


class Builder:
    def __init__(self, S, n_cores=8, debug_dump=False):
        assert S % 256 == 0
        self.S = S
        self.n_cores = n_cores
        self.SSL = []
        o = 0
        while o < S:
            w = min(512, S - o)
            self.SSL.append((o, w))
            o += w
        self.NCH = S // 128
        self.SH = S + 128
        self.debug_dump = debug_dump
        self.nc = bass.Bass("TRN2", target_bir_lowering=False, debug=False,
                            detect_race_conditions=False)
        self.ops = []
        self._declare_dram()
        self._alloc_sbuf()
        pa = self.nc.alloc_psum_tensor
        self.ps0 = pa("ps0", [128, 2048], F32)
        self.ps1 = pa("ps1", [128, 2048], F32)
        nc = self.nc
        self.qS = Ctr(nc, "qS")
        self.qA = Ctr(nc, "qA")
        self.qG = Ctr(nc, "qG")
        self.tS = Ctr(nc, "tS")
        self.vS = Ctr(nc, "vS")
        self.sS = Ctr(nc, "sS")
        self.gS = Ctr(nc, "gS")
        self.cS = Ctr(nc, "cS")

    def te(self, fn): self.ops.append(("tensor", fn))
    def ve(self, fn): self.ops.append(("vector", fn))
    def se(self, fn): self.ops.append(("scalar", fn))
    def sy(self, fn): self.ops.append(("sync", fn))
    def gp(self, fn): self.ops.append(("gpsimd", fn))

    def barrier(self, eng="sync"):
        w = [(self.qS.h, self.qS.n), (self.qA.h, self.qA.n), (self.qG.h, self.qG.n),
             (self.tS.h, self.tS.n), (self.vS.h, self.vS.n), (self.sS.h, self.sS.n)]

        def fn(e, w=w):
            for h, n in w:
                if n:
                    e.wait_ge(h, n)
        self.ops.append((eng, fn))

    def _declare_dram(self):
        nc, S, NCH, SH = self.nc, self.S, self.NCH, self.SH
        dt = nc.dram_tensor
        self.x_h = dt("x", [DM, SH], F32, kind="ExternalInput")
        self.w_in_h = dt("w_in", [DM, DPROJ], F32, kind="ExternalInput")
        self.conv_w_h = dt("conv_w", [CONV, 4], F32, kind="ExternalInput")
        self.conv_b_h = dt("conv_b", [CONV], F32, kind="ExternalInput")
        self.a_h = dt("a_vec", [NH], F32, kind="ExternalInput")
        self.d_h = dt("d_vec", [NH], F32, kind="ExternalInput")
        self.dtb_h = dt("dt_bias", [NH], F32, kind="ExternalInput")
        self.nw_h = dt("norm_w", [DIN], F32, kind="ExternalInput")
        self.w_out_h = dt("w_out", [DIN, DM], F32, kind="ExternalInput")
        self.wq_h = dt("wq_t", [DM, DM], F32, kind="ExternalInput")
        self.ident_h = dt("c_ident", [128, 128], F32, kind="ExternalInput")
        self.tril_h = dt("c_tril", [128, 128], BF16, kind="ExternalInput")
        self.inde_h = dt("c_inde", [64, NH * 128], BF16, kind="ExternalInput")
        self.sel_h = dt("c_sel", [2, 128], BF16, kind="ExternalInput")
        self.nwp_h = dt("c_nwp", [128, NKI], F32, kind="ExternalInput")
        self.cbp_h = dt("c_cbp", [128, 34], F32, kind="ExternalInput")
        self.cwp_h = dt("c_cwp", [128, 34 * 4], F32, kind="ExternalInput")
        self.dpk_h = dt("c_dpk", [2, NG], F32, kind="ExternalInput")
        self.pm_h = dt("c_pm", [128, 16], F32, kind="ExternalInput")
        self.out_h = dt("out", [DM, S], F32, kind="ExternalOutput")
        self.sp_xn = dt("sp_xn", [NKD, 128, S], F32, kind="Internal")
        self.sp_xc = dt("sp_xc", [34, 128, S], BF16, kind="Internal")
        self.sp_wq = dt("sp_wq", [DM, DM], BF16, kind="Internal")
        self.sp_y = dt("sp_y", [NCH, 128, DIN], BF16, kind="Internal")
        self.sp_yn = dt("sp_yn", [NKI, 128, S], BF16, kind="Internal")
        self.sp_ffn = dt("sp_ffn", [NKD, 128, S], BF16, kind="Internal")
        self.sp_rr = dt("sp_rr", [NCH, 2, NH * 128], BF16, kind="Internal")
        self.sp_ee = dt("sp_ee", [NCH, NH * 128], BF16, kind="Internal")
        self.sp_da = dt("sp_da", [2, NH, S], F32, kind="Internal")
        self.sp_cd = dt("sp_cd", [NH], F32, kind="Internal")
        self.sp_as = dt("sp_as", [S], F32, kind="Internal")
        self.ag_i = dt("ag_i", [128, DIN + NG], F32, kind="Internal")
        self.ag_o = dt("ag_o", [128 * self.n_cores, DIN + NG], F32, kind="Internal")
        self.dbg = {}
        if self.debug_dump:
            for name, shape, dty in [
                ("dbg_xn", [NKD, 128, S], BF16), ("dbg_dt", [2 * NH, S], F32),
                ("dbg_xc", [34, 128, S], BF16), ("dbg_y", [NCH, 128, DIN], BF16),
                ("dbg_yn", [NKI, 128, S], BF16), ("dbg_ffn", [NKD, 128, S], BF16),
                ("dbg_sr", [128, DIN + NG], F32),
                ("dbg_wq", [DM, DM], BF16), ("dbg_xnf", [NKD, 128, S], F32),
                ("dbg_rows", [97, SH], F32), ("dbg_wsp", [128, 12], F32),
                ("dbg_da", [2, NH, S], F32), ("dbg_ee", [NCH, NH * 128], BF16),
                ("dbg_rr", [NCH, 2, NH * 128], BF16),
                ("dbg_xq", [NKD, 128, S], BF16), ("dbg_sf", [128, S], F32),
                ("dbg_gt", [2, 128, DIN], BF16), ("dbg_cbm", [128, 128], BF16),
                ("dbg_ep", [128, DIN], BF16),
            ]:
                self.dbg[name] = dt(name, shape, dty, kind="ExternalOutput")

    def _alloc_sbuf(self):
        nc, S, SH = self.nc, self.S, self.SH
        sb = nc.alloc_sbuf_tensor
        self.c_id = sb("c_id", [128, 128], F32)
        self.c_trl = sb("c_trl", [128, 128], BF16)
        self.sel_bf = sb("sel_bf", [2, 128], BF16)
        self.pm_sb = sb("pm_sb", [128, 16], F32)
        self.ones_bf = sb("ones_bf", [128, 128], BF16)
        self.ones_f = sb("ones_f", [128, 128], F32)
        self.a_col = sb("a_col", [NH, 1], F32)
        self.d_col = sb("d_col", [NH, 1], F32)
        self.dtb_col = sb("dtb_col", [NH, 1], F32)
        self.nw_sb = sb("nw_sb", [128, NKI], F32)
        self.cw_sb = sb("cw_sb", [128, 34 * 4], F32)
        self.cb_sb = sb("cb_sb", [128, 34], F32)
        self.d_hp = sb("d_hp", [128, NG], F32)
        self.rows = sb("rows", [97, SH], F32)
        self.wsp = sb("wsp", [128, 12], F32)
        self.s_run = sb("s_run", [128, DIN], F32)
        self.p_run = sb("p_run", [128, NG], F32)
        self.cdexp = sb("cdexp", [128, NG], F32)
        self.e_tok = sb("e_tok", [128, NH], F32)
        self.e2_tok = sb("e2_tok", [128, NH], BF16)
        self.sml = sb("sml", [128, 64], F32)
        self.dacs_c = sb("dacs_c", [NH, 128], F32)
        self.cp_c = sb("cp_c", [NH, 128], F32)
        self.scr64 = sb("scr64", [NH, 128], F32)
        self.e2w = sb("e2w", [NH, 128], F32)
        self.e2_all = sb("e2_all", [128, (S // 128) * NH], BF16)
        self.cd_all = sb("cd_all", [128, (S // 128) * NG], F32)
        self.rhl = sb("rhl", [128, 128], BF16)
        self.d_lhA = sb("d_lhA", [66, 128], BF16)
        self.d_lhB = sb("d_lhB", [64, 128], BF16)
        self.xnt = sb("xnt", [128, NKD * S], BF16)
        self.xnt_h = sb("xnt_h", [128, NKD * 128], BF16)
        self.fsh = sb("fsh", [128, max(NKD * S, 32768)], BF16)
        self.gsh = sb("gsh", [128, 18816], BF16)
        self.kc = sb("kc", [128, 6], F32)
        nc.const_aps.register(F32, 30.0, self.kc.ap()[:, 0:1])
        nc.const_aps.register(F32, 1.0, self.kc.ap()[:, 1:2])
        nc.const_aps.register(F32, -1.0, self.kc.ap()[:, 2:3])
        nc.const_aps.register(F32, 0.0, self.kc.ap()[:, 3:4])
        nc.const_aps.register(F32, 12582912.0, self.kc.ap()[:, 4:5])
        self.sbuf_left = nc.sbuf_bytes_remaining
        assert self.sbuf_left >= 0, f"SBUF over by {-self.sbuf_left}"

    def fvw(self, off_b, n, dtype=BF16):
        assert off_b % 4 == 0
        if dtype == BF16:
            return self.fsh[:, off_b // 2: off_b // 2 + n]
        return self.fsh[:, off_b // 2: off_b // 2 + 2 * n].bitcast(F32)

    def gv(self, off_b, n, dtype=BF16):
        assert off_b % 4 == 0
        if dtype == BF16:
            return self.gsh[:, off_b // 2: off_b // 2 + n]
        return self.gsh[:, off_b // 2: off_b // 2 + 2 * n].bitcast(F32)

    @property
    def f_gt(self): return self.fvw(0, NG * 128)
    @property
    def f_xtok(self): return self.fvw(8192, DIN)
    @property
    def f_xe2(self): return self.fvw(16384, DIN)
    @property
    def f_prevs(self): return self.fvw(24576, DIN)
    @property
    def f_epat(self): return self.fvw(32768, DIN)
    @property
    def f_btok(self): return self.fvw(40960, 128)
    @property
    def f_drhs(self): return self.fvw(41216, NH * 128)[0:66, :]

    def emit_all(self):
        self.p_setup()
        self.p0_wscale()
        self.barrier("vector"); self.barrier("tensor"); self.barrier("sync")
        self.p1_layernorm()
        self.barrier("tensor"); self.barrier("gpsimd"); self.barrier("sync")
        self.p2a_inproj_conv()
        self.p1_ascale()
        self.barrier("vector"); self.barrier("tensor"); self.barrier("sync")
        self.barrier("scalar")
        self.p3a_states()
        self.p_ag_start()
        self.barrier("vector"); self.barrier("tensor"); self.barrier("sync")
        self.p5q_quant()
        self.barrier("vector"); self.barrier("tensor"); self.barrier("sync")
        self.p5m_ffn()
        self.p_ag_combine()
        self.barrier("vector"); self.barrier("tensor"); self.barrier("sync")
        self.barrier("scalar")
        self.p3b_ssd()
        self.barrier("vector"); self.barrier("tensor"); self.barrier("sync")
        self.barrier("gpsimd"); self.barrier("scalar")
        self.p2b_gate()
        self.barrier("vector"); self.barrier("tensor"); self.barrier("sync")
        self.barrier("gpsimd")
        self.p4_outproj()

    def p_setup(self):
        qS, vS, tS = self.qS, self.vS, self.tS
        q1 = qS.bump(16 * 12)

        def loads(sync, q1=q1):
            for o_, i_ in [
                (self.c_id[:, :], self.ident_h.ap()),
                (self.c_trl[:, :], self.tril_h.ap()),
                (self.sel_bf[:, :], self.sel_h.ap()),
                (self.pm_sb[:, :], self.pm_h.ap()),
                (self.a_col[:, :], self.a_h.ap().unsqueeze(1)),
                (self.d_col[:, :], self.d_h.ap().unsqueeze(1)),
                (self.dtb_col[:, :], self.dtb_h.ap().unsqueeze(1)),
                (self.nw_sb[:, :], self.nwp_h.ap()),
                (self.cw_sb[:, :], self.cwp_h.ap()),
                (self.cb_sb[:, :], self.cbp_h.ap()),
                (self.sml[0:1, 0:NG], self.dpk_h.ap()[0:1, :]),
                (self.sml[32:33, 0:NG], self.dpk_h.ap()[1:2, :]),
            ]:
                sync.dma_start(out=o_, in_=i_).then_inc(qS.h, 16)
        self.sy(loads)

        v1 = vS.bump()

        def inits(v, q1=q1, v1=v1):
            v.wait_ge(qS.h, q1)
            v.memset(self.ones_bf[:, :], 1.0)
            v.memset(self.ones_f[:, :], 1.0)
            v.memset(self.kc[:, 0:1], 30.0)
            v.memset(self.kc[:, 1:2], 1.0)
            v.memset(self.kc[:, 2:3], -1.0)
            v.memset(self.kc[:, 3:4], 0.0)
            v.memset(self.kc[:, 4:5], 12582912.0)
            if self.debug_dump:
                v.memset(self.rows[:, :], 0.0)
                v.memset(self.wsp[:, :], 0.0)
                v.memset(self.sml[:, :], 0.0)
            v.memset(self.d_lhA[64:66, :], 1.0)
            v.memset(self.s_run[:, :], 0.0)
            v.memset(self.p_run[:, :], 1.0).then_inc(vS.h, 1)
        self.ve(inits)

        t1 = tS.bump()

        def dhp_mm(t, v1=v1, t1=t1):
            t.wait_ge(vS.h, v1)
            t.matmul(self.ps1[0:64, 0:NG], self.ones_f[0:1, 0:64], self.sml[0:1, 0:NG],
                     start=True, stop=True)
            t.matmul(self.ps1[64:128, 0:NG], self.ones_f[32:33, 0:64],
                     self.sml[32:33, 0:NG], start=True, stop=True).then_inc(tS.h, 1)
        self.te(dhp_mm)

        v2 = vS.bump()

        def dhp_evac(v, t1=t1):
            v.wait_ge(tS.h, t1)
            v.tensor_copy(self.d_hp[:, :], self.ps1[:, 0:NG]).then_inc(vS.h, 1)
        self.ve(dhp_evac)

    def p0_wscale(self):
        qS, vS, tS, gS = self.qS, self.vS, self.tS, self.gS
        bufs = [self.gv(0, 2048, F32), self.gv(8192, 2048, F32)]
        acc = self.wsp[:, 0:1]
        ve_done = {}
        for i in range(NKD):
            buf = bufs[i % 2]
            wv = ve_done.get(i - 2)
            qt = qS.bump(16)

            def ld(sync, i=i, buf=buf, wv=wv):
                if wv:
                    sync.wait_ge(vS.h, wv)
                sync.dma_start(out=buf,
                               in_=self.wq_h.ap()[i * 128:(i + 1) * 128, :]).then_inc(qS.h, 16)
            self.sy(ld)
            vt = vS.bump()

            def red(v, i=i, buf=buf, qt=qt):
                v.wait_ge(qS.h, qt)
                dst = self.sml[:, 8 + (i % 2):9 + (i % 2)]
                v.tensor_reduce(dst, buf, AxX, Alu.add, apply_absolute_value=True)
                v.drain()
                if i == 0:
                    v.tensor_copy(acc, dst).then_inc(vS.h, 1)
                else:
                    v.tensor_add(acc, acc, dst).then_inc(vS.h, 1)
            self.ve(red)
            ve_done[i] = vt

        vall = vS.n
        t1 = tS.bump()

        def pred(t, vall=vall):
            t.wait_ge(vS.h, vall)
            t.matmul(self.ps1[0:1, 64:65], self.ones_f[:, 0:1], acc,
                     start=True, stop=True).then_inc(tS.h, 1)
        self.te(pred)
        v1 = vS.bump()

        def wsfin(v, t1=t1):
            v.wait_ge(tS.h, t1)
            v.tensor_scalar(self.wsp[0:1, 1:2], self.ps1[0:1, 64:65], 1.0 / (DM * DM), 1e-6,
                            Alu.mult, Alu.max)
            v.drain()
            v.reciprocal(self.wsp[0:1, 2:3], self.wsp[0:1, 1:2]).then_inc(vS.h, 1)
        self.ve(wsfin)
        t2b = tS.bump()

        def winv_mm(t, v1=v1):
            t.wait_ge(vS.h, v1)
            t.matmul(self.ps1[:, 65:66], self.ones_f[0:1, :], self.wsp[0:1, 2:3],
                     start=True, stop=True)
            t.matmul(self.ps1[:, 66:67], self.ones_f[0:1, :], self.wsp[0:1, 1:2],
                     start=True, stop=True).then_inc(tS.h, 1)
        self.te(winv_mm)
        g1 = vS.bump()

        def winv_ev(v, t2b=t2b):
            v.wait_ge(tS.h, t2b)
            v.tensor_copy(self.wsp[:, 3:4], self.ps1[:, 65:66])
            v.tensor_copy(self.wsp[:, 5:6], self.ps1[:, 66:67]).then_inc(vS.h, 1)
        self.ve(winv_ev)

        q_bufs = [self.gv(16384, 2048, BF16), self.gv(20480, 2048, BF16)]
        t_buf = self.gv(24576, 2048, F32)
        f_buf = self.gv(32768, 1024, F32)
        st_done = {}
        for i in range(NKD):
            buf = bufs[i % 2]
            qb = q_bufs[i % 2]
            wv = st_done.get(i - 2)
            qt = qS.bump(16)

            def ld(sync, i=i, buf=buf, wv=wv, g1=g1):
                if i == 0:
                    sync.wait_ge(vS.h, g1)
                if wv:
                    sync.wait_ge(qS.h, wv)
                sync.dma_start(out=buf,
                               in_=self.wq_h.ap()[i * 128:(i + 1) * 128, :]).then_inc(qS.h, 16)
            self.sy(ld)
            vt = vS.bump()

            def quant(v, buf=buf, qb=qb, qt=qt):
                v.wait_ge(qS.h, qt)
                for hb in range(2):
                    sl = slice(hb * 1024, (hb + 1) * 1024)
                    tb = t_buf[:, sl]
                    v.scalar_tensor_tensor(tb, buf[:, sl], self.wsp[:, 3:4], buf[:, sl],
                                           Alu.mult, Alu.bypass)
                    v.tensor_scalar_add(f_buf, tb, 12582912.0)
                    v.tensor_scalar_add(tb, f_buf, -12582912.0)
                v.tensor_scalar(qb, t_buf, 1.0, -1.0, Alu.min, Alu.max).then_inc(vS.h, 1)
            self.ve(quant)
            st = qS.bump(16)

            def stf(sync, i=i, qb=qb, vt=vt):
                sync.wait_ge(vS.h, vt)
                sync.dma_start(out=self.sp_wq.ap()[i * 128:(i + 1) * 128, :],
                               in_=qb).then_inc(qS.h, 16)
            self.sy(stf)
            st_done[i] = st

    def p1_layernorm(self):
        S, SH = self.S, self.SH
        qS, vS, tS, sS = self.qS, self.vS, self.tS, self.sS
        xld = [self.gv(0, SH, F32), self.gv(4 * SH, SH, F32)]
        ve_done = {}
        for k in range(NKD):
            buf = xld[k % 2]
            wv = ve_done.get(k - 2)
            qt = qS.bump(16)

            def ld(sync, k=k, buf=buf, wv=wv):
                if wv:
                    sync.wait_ge(vS.h, wv)
                sync.dma_start(out=buf,
                               in_=self.x_h.ap()[k * 128:(k + 1) * 128, :]).then_inc(qS.h, 16)
            self.sy(ld)
            vt = vS.bump()

            def park(v, k=k, buf=buf, qt=qt):
                v.wait_ge(qS.h, qt)
                v.tensor_copy(self.xnt_h[:, k * 128:(k + 1) * 128], buf[:, 0:128])
                v.tensor_copy(self.fsh[:, k * S:(k + 1) * S], buf[:, 128:SH]).then_inc(vS.h, 1)
            self.ve(park)
            ve_done[k] = vt

        pall = vS.n
        t1 = tS.bump()

        def sums(t, pall=pall):
            t.wait_ge(vS.h, pall)
            for k in range(NKD):
                for (o, w) in self.SSL:
                    ins = t.matmul(self.ps0[0:1, o:o + w], self.ones_bf[:, 0:1],
                                   self.fsh[:, k * S + o: k * S + o + w],
                                   start=(k == 0), stop=(k == NKD - 1))
                ins = t.matmul(self.ps1[0:1, 0:128], self.ones_bf[:, 0:1],
                               self.xnt_h[:, k * 128:(k + 1) * 128],
                               start=(k == 0), stop=(k == NKD - 1))
            ins.then_inc(tS.h, 1)
        self.te(sums)
        v1 = vS.bump()

        def sum_evac(v, t1=t1):
            v.wait_ge(tS.h, t1)
            v.tensor_copy(self.rows[0:1, 128:SH], self.ps0[0:1, 0:S])
            v.tensor_copy(self.rows[0:1, 0:128], self.ps1[0:1, 0:128]).then_inc(vS.h, 1)
        self.ve(sum_evac)
        s1 = sS.bump()

        def squares(s, v1=v1):
            s.wait_ge(vS.h, v1)
            for k in range(NKD):
                s.activation(self.fsh[:, k * S:(k + 1) * S],
                             self.fsh[:, k * S:(k + 1) * S], Act.Square)
            s.activation(self.xnt_h[:, :], self.xnt_h[:, :], Act.Square).then_inc(sS.h, 1)
        self.se(squares)
        t2 = tS.bump()

        def sqsums(t, s1=s1):
            t.wait_ge(sS.h, s1)
            for k in range(NKD):
                for (o, w) in self.SSL:
                    ins = t.matmul(self.ps0[0:1, o:o + w], self.ones_bf[:, 0:1],
                                   self.fsh[:, k * S + o: k * S + o + w],
                                   start=(k == 0), stop=(k == NKD - 1))
                ins = t.matmul(self.ps1[0:1, 0:128], self.ones_bf[:, 0:1],
                               self.xnt_h[:, k * 128:(k + 1) * 128],
                               start=(k == 0), stop=(k == NKD - 1))
            ins.then_inc(tS.h, 1)
        self.te(sqsums)
        v2 = vS.bump()

        def fin_rows(v, t2=t2):
            v.wait_ge(tS.h, t2)
            v.tensor_copy(self.rows[32:33, 128:SH], self.ps0[0:1, 0:S])
            v.tensor_copy(self.rows[32:33, 0:128], self.ps1[0:1, 0:128]).then_inc(vS.h, 1)
        self.ve(fin_rows)
        m_sb = self.fvw(0, SH, F32)
        msq = self.fvw(4 * SH, SH, F32)
        rstd_f = self.fvw(8 * SH, SH, F32)
        r2_f = self.fvw(12 * SH, SH, F32)
        v2b = vS.n
        t3 = tS.bump()

        def bc_m(t, v2b=v2b):
            t.wait_ge(vS.h, v2b)
            for (o, w) in self.SSL:
                t.matmul(self.ps0[:, o:o + w], self.ones_f[0:1, :],
                         self.rows[0:1, 128 + o:128 + o + w], start=True, stop=True)
            t.matmul(self.ps1[:, 0:128], self.ones_f[0:1, :],
                     self.rows[0:1, 0:128], start=True, stop=True).then_inc(tS.h, 1)
        self.te(bc_m)
        v2c = vS.bump()

        def ev_m(v, t3=t3):
            v.wait_ge(tS.h, t3)
            v.tensor_scalar_mul(m_sb[:, 128:SH], self.ps0[:, 0:S], 1.0 / DM)
            v.tensor_scalar_mul(m_sb[:, 0:128], self.ps1[:, 0:128], 1.0 / DM)
            v.drain()
            v.tensor_tensor(msq, m_sb, m_sb, Alu.mult).then_inc(vS.h, 1)
        self.ve(ev_m)
        t4 = tS.bump()

        def bc_q(t, v2c=v2c):
            t.wait_ge(vS.h, v2c)
            for (o, w) in self.SSL:
                t.matmul(self.ps0[:, o:o + w], self.ones_f[32:33, :],
                         self.rows[32:33, 128 + o:128 + o + w], start=True, stop=True)
            t.matmul(self.ps1[:, 0:128], self.ones_f[32:33, :],
                     self.rows[32:33, 0:128], start=True, stop=True).then_inc(tS.h, 1)
        self.te(bc_q)
        v2d = vS.bump()

        def var_f(v, t4=t4):
            v.wait_ge(tS.h, t4)
            v.scalar_tensor_tensor(rstd_f[:, 128:SH], self.ps0[:, 0:S], 1.0 / DM,
                                   msq[:, 128:SH], Alu.mult, Alu.subtract)
            v.scalar_tensor_tensor(rstd_f[:, 0:128], self.ps1[:, 0:128], 1.0 / DM,
                                   msq[:, 0:128], Alu.mult, Alu.subtract)
            v.drain()
            v.tensor_scalar_add(rstd_f, rstd_f, EPS).then_inc(vS.h, 1)
        self.ve(var_f)
        s2 = sS.bump()

        def rstd_act(s, v2d=v2d):
            s.wait_ge(vS.h, v2d)
            s.activation(rstd_f, rstd_f, Act.Sqrt).then_inc(sS.h, 1)
        self.se(rstd_act)
        v3 = vS.bump()

        def hilo_rows(v, s2=s2):
            v.wait_ge(sS.h, s2)
            v.reciprocal(rstd_f, rstd_f)
            v.scalar_tensor_tensor(r2_f, m_sb, -1.0, rstd_f, Alu.mult, Alu.mult)
            v.memset(self.fvw(16 * SH + 4 * S, S, F32), 0.0)
            v.memset(self.fvw(16 * SH, S, F32), 0.0).then_inc(vS.h, 1)
        self.ve(hilo_rows)

        amax = self.fvw(16 * SH + 4 * S, S, F32)
        amin = self.fvw(16 * SH, S, F32)
        xn_f = self.gv(8 * SH + 4 * S, SH, F32)
        half = SH // 2

        def _slices(hw):
            out, o = [], 0
            while o < hw:
                w = min(512, hw - o)
                out.append((o, w))
                o += w
            return out

        for hf in range(2):
            base = hf * half
            for k in range(NKD):
                buf = xld[k % 2]
                wv = vS.n if k >= 1 else None
                qt = qS.bump(16)

                def ld(sync, k=k, buf=buf, base=base, wv=wv):
                    if wv and k >= 2:
                        sync.wait_ge(vS.h, wv)
                    sync.dma_start(out=buf[:, 0:half],
                                   in_=self.x_h.ap()[k * 128:(k + 1) * 128,
                                                     base:base + half]).then_inc(qS.h, 16)
                self.sy(ld)
                vt = vS.bump()

                def xn_ops(v, k=k, buf=buf, hf=hf, base=base, qt=qt):
                    v.wait_ge(qS.h, qt)
                    v.tensor_tensor(xn_f[:, 0:half], buf[:, 0:half],
                                    rstd_f[:, base:base + half], Alu.mult)
                    v.tensor_tensor(xn_f[:, 0:half], xn_f[:, 0:half],
                                    r2_f[:, base:base + half], Alu.add)
                    if hf == 0:
                        v.tensor_copy(self.xnt_h[:, k * 128:(k + 1) * 128], xn_f[:, 0:128])
                        v.tensor_copy(self.xnt[:, k * S: k * S + (half - 128)], xn_f[:, 128:half])
                        v.tensor_tensor(amax[:, 0:half - 128], amax[:, 0:half - 128],
                                        xn_f[:, 128:half], Alu.max)
                        v.tensor_tensor(amin[:, 0:half - 128], amin[:, 0:half - 128],
                                        xn_f[:, 128:half], Alu.min).then_inc(vS.h, 1)
                    else:
                        mo = half - 128
                        v.tensor_copy(self.xnt[:, k * S + mo: k * S + mo + half],
                                      xn_f[:, 0:half])
                        v.tensor_tensor(amax[:, mo:mo + half], amax[:, mo:mo + half],
                                        xn_f[:, 0:half], Alu.max)
                        v.tensor_tensor(amin[:, mo:mo + half], amin[:, mo:mo + half],
                                        xn_f[:, 0:half], Alu.min).then_inc(vS.h, 1)
                self.ve(xn_ops)
                st = qS.bump(16)

                def spill(sync, k=k, hf=hf, vt=vt):
                    sync.wait_ge(vS.h, vt)
                    if hf == 0:
                        ap = self.sp_xn.ap()[k, :, 0:half - 128]
                        src = xn_f[:, 128:half]
                    else:
                        ap = self.sp_xn.ap()[k, :, half - 128: 2 * half - 128]
                        src = xn_f[:, 0:half]
                    sync.dma_start(out=ap, in_=src).then_inc(qS.h, 16)
                self.sy(spill)
                g1 = vS.bump()

                def guard(v, st=st):
                    v.wait_ge(qS.h, st)
                    v.engine_nop().then_inc(vS.h, 1)
                self.ve(guard)

        self._amax = amax
        self._amin = amin

    def p1_ascale(self):
        S, SH, NCH = self.S, self.SH, self.NCH
        qS, vS, tS, sS = self.qS, self.vS, self.tS, self.sS
        va = vS.bump()

        def as_comb(v):
            v.tensor_scalar_mul(self._amin, self._amin, -1.0)
            v.tensor_tensor(self._amax, self._amax, self._amin,
                            Alu.max).then_inc(vS.h, 1)
        self.ve(as_comb)
        NCH = self.NCH
        ta = tS.bump()

        def as_tp(t, va=va):
            t.wait_ge(vS.h, va)
            for c in range(NCH):
                ins = t.transpose(self.ps1[:, c * 128:(c + 1) * 128],
                                  self._amax[:, c * 128:(c + 1) * 128], self.c_id[:, :])
            ins.then_inc(tS.h, 1)
        self.te(as_tp)
        vb = vS.bump()

        def as_red(v, ta=ta):
            v.wait_ge(tS.h, ta)
            for c in range(NCH):
                ins = v.tensor_reduce(self.sml[:, 48 + c:49 + c],
                                      self.ps1[:, c * 128:(c + 1) * 128],
                                      AxX, Alu.max)
            ins.then_inc(vS.h, 1)
        self.ve(as_red)
        qa1 = qS.bump(16 * NCH + 16)

        def as_bounce(sync, vb=vb):
            sync.wait_ge(vS.h, vb)
            for c in range(NCH):
                sync.dma_start(out=self.sp_as.ap()[c * 128:(c + 1) * 128].unsqueeze(1),
                               in_=self.sml[:, 48 + c:49 + c]).then_inc(qS.h, 16)
            sync.dma_start(out=self.rows[64:65, 128:SH],
                           in_=self.sp_as.ap().unsqueeze(0)).then_inc(qS.h, 16)
        self.sy(as_bounce)
        vc = vS.bump()

        def as_fin(v, qa1=qa1):
            v.wait_ge(qS.h, qa1)
            v.tensor_scalar_max(self.rows[64:65, 128:SH], self.rows[64:65, 128:SH],
                                1e-8).then_inc(vS.h, 1)
        self.ve(as_fin)

        if self.debug_dump:
            qd = qS.bump(16 * NKD)

            def dump(sync, va=va):
                sync.wait_ge(vS.h, va)
                for k in range(NKD):
                    sync.dma_start(out=self.dbg["dbg_xn"].ap()[k],
                                   in_=self.xnt[:, k * S:(k + 1) * S]).then_inc(qS.h, 16)
            self.sy(dump)


    def p2a_inproj_conv(self):
        S, SH = self.S, self.SH
        qS, qG, vS, tS, sS = self.qS, self.qG, self.vS, self.tS, self.sS
        wb = [self.gv(0, 2048, BF16), self.gv(4096, 2048, BF16)]
        xbc_w = self.gv(8192, SH, F32)
        xcs = [self.gv(8192 + 4 * SH, S, BF16), self.gv(8192 + 4 * SH + 2 * S, S, BF16)]
        acc_b = 8192 + 4 * SH + 4 * S
        accv = self.gv(acc_b, S, F32)
        sgv = self.gv(acc_b + 4 * S, S, BF16)
        mm_done = {}
        ev_done = {}
        st_done = {}
        for m in range(35):
            buf = wb[m % 2]
            ncols = 128 if m < 34 else 64
            coff = 4096 + 128 * m
            wprev = mm_done.get(m - 2)
            qw = qG.bump(16)

            def wld(g, buf=buf, ncols=ncols, coff=coff, wprev=wprev):
                if wprev:
                    g.wait_ge(tS.h, wprev)
                wap = AP(self.w_in_h, coff, [[DPROJ, 128], [DPROJ * 128, NKD], [1, ncols]])
                g.dma_start(out=buf[:, 0:NKD * ncols].rearrange("p (k c) -> p k c", k=NKD),
                            in_=wap).then_inc(qG.h, 16)
            self.gp(wld)
            evp = ev_done.get(m - 1)
            tm = tS.bump()

            def mms(t, buf=buf, ncols=ncols, qw=qw, evp=evp):
                t.wait_ge(qG.h, qw)
                if evp:
                    t.wait_ge(vS.h, evp)
                rows = slice(0, ncols)
                for k in range(NKD):
                    lhs = buf[:, k * ncols:(k + 1) * ncols]
                    st = (k == 0)
                    sp = (k == NKD - 1)
                    t.matmul(self.ps1[rows, 0:128], lhs,
                             self.xnt_h[:, k * 128:(k + 1) * 128], start=st, stop=sp)
                    for (o, w) in self.SSL:
                        ins = t.matmul(self.ps0[rows, o:o + w], lhs,
                                       self.xnt[:, k * S + o: k * S + o + w],
                                       start=st, stop=sp)
                ins.then_inc(tS.h, 1)
            self.te(mms)
            mm_done[m] = tm

            if m < 34:
                ob = xcs[m % 2]
                ve1 = vS.bump()

                def evac(v, tm=tm):
                    v.wait_ge(tS.h, tm)
                    v.tensor_copy(xbc_w[:, 0:128], self.ps1[:, 0:128])
                    v.tensor_copy(xbc_w[:, 128:SH], self.ps0[:, 0:S]).then_inc(vS.h, 1)
                self.ve(evac)
                ev_done[m] = ve1
                stp = st_done.get(m - 2)
                ve2 = vS.bump()

                def conv(v, m=m, stp=stp):
                    if stp:
                        v.wait_ge(qS.h, stp)
                    w4 = self.cw_sb[:, m * 4: m * 4 + 4]
                    v.tensor_scalar_mul(accv, xbc_w[:, 125:125 + S], w4[:, 0:1])
                    v.scalar_tensor_tensor(accv, xbc_w[:, 126:126 + S], w4[:, 1:2], accv,
                                           Alu.mult, Alu.add)
                    v.scalar_tensor_tensor(accv, xbc_w[:, 127:127 + S], w4[:, 2:3], accv,
                                           Alu.mult, Alu.add)
                    v.scalar_tensor_tensor(accv, xbc_w[:, 128:128 + S], w4[:, 3:4], accv,
                                           Alu.mult, Alu.add)
                    v.tensor_scalar_add(accv, accv, self.cb_sb[:, m:m + 1]).then_inc(vS.h, 1)
                self.ve(conv)
                s1 = sS.bump()

                def sig(s, ve2=ve2):
                    s.wait_ge(vS.h, ve2)
                    s.activation(sgv, accv, Act.Sigmoid).then_inc(sS.h, 1)
                self.se(sig)
                ve3 = vS.bump()

                def silu(v, ob=ob, s1=s1):
                    v.wait_ge(sS.h, s1)
                    v.tensor_tensor(ob, accv, sgv, Alu.mult).then_inc(vS.h, 1)
                self.ve(silu)
                st = qS.bump(16)

                def store(sync, m=m, ob=ob, ve3=ve3):
                    sync.wait_ge(vS.h, ve3)
                    sync.dma_start(out=self.sp_xc.ap()[m], in_=ob).then_inc(qS.h, 16)
                self.sy(store)
                st_done[m] = st
            else:
                u = self.gv(4096, S, F32)[0:64, :]
                e = self.gv(12288, S, F32)[0:64, :]
                dacs = self.gv(20480, S, F32)[0:64, :]
                dtt = self.gv(28672, S, F32)[0:64, :]
                one_t = self.gv(36864, 128, F32)[0:64, :]
                v1 = vS.bump()

                def dt_v1(v, tm=tm):
                    v.wait_ge(tS.h, tm)
                    v.tensor_scalar(u, self.ps0[0:64, 0:S], self.dtb_col[:, :], 30.0,
                                    Alu.add, Alu.min).then_inc(vS.h, 1)
                self.ve(dt_v1)
                s1 = sS.bump()

                def dt_s1(s, v1=v1):
                    s.wait_ge(vS.h, v1)
                    s.activation(e, u, Act.Exp)
                    s.activation(dtt, e, Act.Ln, bias=1.0).then_inc(sS.h, 1)
                self.se(dt_s1)
                v2 = vS.bump()

                def dt_v2(v, s1=s1):
                    s = s1
                    v.wait_ge(sS.h, s)
                    v.tensor_scalar_max(e, dtt, 1e-38).then_inc(vS.h, 1)
                self.ve(dt_v2)
                s2 = sS.bump()

                def dt_s2(s, v2=v2):
                    s.wait_ge(vS.h, v2)
                    s.activation(e, e, Act.Ln).then_inc(sS.h, 1)
                self.se(dt_s2)
                v3 = vS.bump()

                def dt_v3(v, s2=s2):
                    v.wait_ge(sS.h, s2)
                    v.tensor_scalar_mul(u, dtt, self.a_col[:, :])
                    v.memset(one_t, 1.0)
                    v.drain()
                    for j in range(self.NCH):
                        v.tensor_tensor_scan(dacs[:, j * 128:(j + 1) * 128], one_t,
                                             u[:, j * 128:(j + 1) * 128], 0.0,
                                             Alu.mult, Alu.add)
                    v.drain()
                    v.tensor_sub(e, dacs, e).then_inc(vS.h, 1)
                self.ve(dt_v3)
                qd = qS.bump(32 if not self.debug_dump else 64)

                def dt_spill(sync, v3=v3):
                    sync.wait_ge(vS.h, v3)
                    sync.dma_start(out=self.sp_da.ap()[0], in_=dacs).then_inc(qS.h, 16)
                    sync.dma_start(out=self.sp_da.ap()[1], in_=e).then_inc(qS.h, 16)
                    if self.debug_dump:
                        sync.dma_start(out=self.dbg["dbg_dt"].ap()[0:NH, :],
                                       in_=dtt).then_inc(qS.h, 16)
                        sync.dma_start(out=self.dbg["dbg_dt"].ap()[NH:2 * NH, :],
                                       in_=dacs).then_inc(qS.h, 16)
                self.sy(dt_spill)

    def p5q_quant(self):
        S, SH = self.S, self.SH
        qS, vS, tS, sS = self.qS, self.vS, self.tS, self.sS
        sbc = self.gv(12 * 2048, 2048, F32)
        vt0 = vS.n
        t1 = tS.bump()

        def asbc(t, vt0=vt0):
            t.wait_ge(vS.h, vt0)
            for (o, w) in self.SSL:
                ins = t.matmul(self.ps0[:, o:o + w], self.ones_f[64:65, :],
                               self.rows[64:65, 128 + o:128 + o + w],
                               start=True, stop=True)
            ins.then_inc(tS.h, 1)
        self.te(asbc)
        v1 = vS.bump()

        def sbuild(v, t1=t1):
            v.wait_ge(tS.h, t1)
            v.tensor_copy(sbc[:, 0:S], self.ps0[:, 0:S])
            v.reciprocal(sbc[:, 0:S], sbc[:, 0:S])
            v.tensor_scalar_mul(sbc[:, 0:S], sbc[:, 0:S], 127.0).then_inc(vS.h, 1)
        self.ve(sbuild)

        xbufs = [self.gv(0, S, F32), self.gv(4 * S, S, F32)]
        tbuf = self.gv(8 * S, S, F32)
        qdone = {}
        for k in range(NKD):
            xbuf = xbufs[k % 2]
            qt = qS.bump(16)
            pv = qdone.get(k - 2)

            def ld(sync, k=k, xbuf=xbuf, pv=pv):
                if pv:
                    sync.wait_ge(vS.h, pv)
                sync.dma_start(out=xbuf, in_=self.sp_xn.ap()[k]).then_inc(qS.h, 16)
            self.sy(ld)
            vt = vS.bump()

            def quant(v, k=k, xbuf=xbuf, qt=qt, v1=v1):
                v.wait_ge(qS.h, qt)
                v.wait_ge(vS.h, v1)
                v.tensor_tensor(tbuf, xbuf, sbc[:, 0:S], Alu.mult)
                v.tensor_scalar_add(tbuf, tbuf, 12582912.0)
                v.tensor_scalar_add(tbuf, tbuf, -12582912.0)
                v.tensor_copy(self.fsh[:, k * S:(k + 1) * S], tbuf).then_inc(vS.h, 1)
            self.ve(quant)
            qdone[k] = vt
            if self.debug_dump:
                qS.bump(16)

                def dxq(sync, k=k, vt=vt):
                    sync.wait_ge(vS.h, vt)
                    sync.dma_start(out=self.dbg["dbg_xq"].ap()[k],
                                   in_=self.fsh[:, k * S:(k + 1) * S]).then_inc(qS.h, 16)
                self.sy(dxq)
        self._p5q_marks = [qdone[k] for k in range(NKD)]

    def _chunk_loads(self, j, with_c, guards=None, c_dst=None):
        S, qS = self.S, self.qS
        xcj = self.gv(0, DIN, BF16)
        if c_dst is None:
            c_dst = self.gv(8448, 128, BF16)
        n = 5 if with_c else 4
        if guards is None:
            guards = [(self.vS.h, self.vS.n), (self.sS.h, self.sS.n),
                      (self.tS.h, self.tS.n), (self.qA.h, self.qA.n)]
        qt = qS.bump(16 * n)

        def ld(sync, j=j, with_c=with_c, guards=guards, qt=qt, n=n):
            for h, c in guards:
                if c:
                    sync.wait_ge(h, c)
            ap = AP(self.sp_xc, j * 128, [[S, 128], [128 * S, 32], [1, 128]])
            sync.dma_start(out=xcj.rearrange("p (g c) -> p g c", g=32), in_=ap).then_inc(qS.h, 16)
            bap = AP(self.sp_xc, 32 * 128 * S + j * 128, [[S, 128], [1, 128]])
            sync.dma_start(out=self.gv(8192, 128, BF16), in_=bap).then_inc(qS.h, 16)
            if with_c:
                cap = AP(self.sp_xc, 33 * 128 * S + j * 128, [[S, 128], [1, 128]])
                sync.dma_start(out=c_dst, in_=cap).then_inc(qS.h, 16)
            dap = AP(self.sp_da, j * 128, [[S, NH], [1, 128]])
            sync.dma_start(out=self.dacs_c[:, :], in_=dap).then_inc(qS.h, 16)
            cpap = AP(self.sp_da, NH * S + j * 128, [[S, NH], [1, 128]])
            sync.dma_start(out=self.cp_c[:, :], in_=cpap).then_inc(qS.h, 16)
        self.sy(ld)
        return qt

    def _chunk_tposes(self, j, ldtgt, tpose_waits=()):
        qA = self.qA
        xcj = self.gv(0, DIN, BF16)
        b_sl = self.gv(8192, 128, BF16)
        qt = qA.bump(32)

        def tposes(sc, ldtgt=ldtgt, tpose_waits=tuple(tpose_waits)):
            sc.wait_ge(self.qS.h, ldtgt)
            for h, v in tpose_waits:
                sc.wait_ge(h, v)
            sc.dma_start(out=self.f_xtok.rearrange("p (g c) -> p g c", g=32),
                         in_=xcj.rearrange("p (g c) -> p g c", g=32),
                         transpose=True).then_inc(qA.h, 16)
            sc.dma_start(out=self.f_btok, in_=b_sl, transpose=True).then_inc(qA.h, 16)
        self.se(tposes)
        return qt

    def _chunk_common(self, j, ldtgt, tpose_waits=(), psum_free=None):
        qA, qS, vS, tS, sS = self.qA, self.qS, self.vS, self.tS, self.sS
        xcj = self.gv(0, DIN, BF16)
        b_sl = self.gv(8192, 128, BF16)
        qt = qA.bump(32)

        def tposes(sc, ldtgt=ldtgt, tpose_waits=tuple(tpose_waits)):
            sc.wait_ge(self.qS.h, ldtgt)
            for h, v in tpose_waits:
                sc.wait_ge(h, v)
            sc.dma_start(out=self.f_xtok.rearrange("p (g c) -> p g c", g=32),
                         in_=xcj.rearrange("p (g c) -> p g c", g=32),
                         transpose=True).then_inc(qA.h, 16)
            sc.dma_start(out=self.f_btok, in_=b_sl, transpose=True).then_inc(qA.h, 16)
        self.se(tposes)
        s0 = sS.bump()

        def cdexp_a(s, ldtgt=ldtgt):
            s.wait_ge(self.qS.h, ldtgt)
            s.activation(self.sml[0:64, 48:49], self.dacs_c[:, 127:128],
                         Act.Exp).then_inc(sS.h, 1)
        self.se(cdexp_a)
        tcd = tS.n
        qcd = qS.bump(32)

        def cd_bounce(sync, s0=s0):
            sync.wait_ge(sS.h, s0)
            sync.dma_start(out=self.sp_cd.ap().unsqueeze(1),
                           in_=self.sml[0:64, 48:49]).then_inc(qS.h, 16)
            sync.dma_start(out=self.rows[0:1, 0:NH],
                           in_=self.sp_cd.ap().unsqueeze(0)).then_inc(qS.h, 16)
        self.sy(cd_bounce)
        vcd = qcd
        v1n = vS.bump()

        def negE2_v(v, ldtgt=ldtgt):
            v.wait_ge(self.qS.h, ldtgt)
            v.tensor_scalar_sub(self.e2w[:, :], self.cp_c[:, :],
                                self.dacs_c[:, 127:128]).then_inc(vS.h, 1)
        self.ve(negE2_v)
        t1 = tS.bump()

        def petr(t, v1n=v1n, psum_free=psum_free):
            t.wait_ge(vS.h, v1n)
            if psum_free is not None:
                t.wait_ge(vS.h, psum_free)
            t.transpose(self.ps1[:, 1600:1664], self.e2w[:, :],
                        self.c_id[0:64, 0:64]).then_inc(tS.h, 1)
        self.te(petr)
        s1 = sS.bump()

        def acts(s, t1=t1, j=j):
            s.wait_ge(tS.h, t1)
            s.activation(self.e2_all[:, j * NH:(j + 1) * NH],
                         self.ps1[:, 1600:1664], Act.Exp,
                         scale=-1.0).then_inc(sS.h, 1)
        self.se(acts)
        t2 = tS.bump()

        def cdmm(t, qcd=qcd):
            t.wait_ge(qS.h, qcd)
            ev = self.rows[0:1, 0:NH].rearrange("a (g r) -> a g r", r=2)
            t.matmul(self.ps1[0:64, 1664:1696], self.ones_f[0:1, 0:64],
                     ev[:, :, 0:1].squeeze(2), start=True, stop=True)
            t.matmul(self.ps1[64:128, 1664:1696], self.ones_f[0:1, 0:64],
                     ev[:, :, 1:2].squeeze(2), start=True, stop=True).then_inc(tS.h, 1)
        self.te(cdmm)
        v4 = vS.bump()

        def cdev(v, t2=t2, j=j):
            v.wait_ge(tS.h, t2)
            cdj = self.cd_all[:, j * NG:(j + 1) * NG]
            v.tensor_copy(cdj, self.ps1[:, 1664:1696])
            v.drain()
            v.tensor_mul(self.p_run[:, :], self.p_run[:, :],
                         cdj).then_inc(vS.h, 1)
        self.ve(cdev)
        return qt, v4, s1, {"qA": qt, "sE": s0, "vN": v1n, "t2": t2}

    def _states_scan(self, j, ttgt, cdt, e2s, s_pre=None, ps_free_s=None):
        vS, tS, qA, sS = self.vS, self.tS, self.qA, self.sS
        v1 = vS.bump()

        def mkxe2(v, ttgt=ttgt, cdt=cdt, e2s=e2s, j=j):
            v.wait_ge(qA.h, ttgt)
            if cdt is not None:
                v.wait_ge(vS.h, cdt)
            if e2s is not None:
                v.wait_ge(sS.h, e2s)
            e2b = self.e2_all[:, j * NH:(j + 1) * NH].unsqueeze(2).broadcast_to(
                [128, NH, HD])
            v.tensor_tensor(self.f_xe2.rearrange("p (h d) -> p h d", h=NH),
                            self.f_xtok.rearrange("p (h d) -> p h d", h=NH),
                            e2b, Alu.mult).then_inc(vS.h, 1)
        self.ve(mkxe2)
        tbase = self.tS.n
        tmarks = [tS.bump() for _ in range(4)]

        def smms(t, v1=v1, ps_free_s=ps_free_s):
            t.wait_ge(vS.h, v1)
            if ps_free_s is not None:
                t.wait_ge(sS.h, ps_free_s)
            for g in range(NG):
                ps = self.ps0 if g < 16 else self.ps1
                ins = t.matmul(ps[:, (g % 16) * 128:(g % 16) * 128 + 128],
                               self.f_xe2[:, g * 128:(g + 1) * 128], self.f_btok,
                               start=True, stop=True)
                if g % 8 == 7:
                    ins.then_inc(tS.h, 1)
        self.te(smms)
        v2 = vS.bump()

        def scan(v, tmarks=tmarks, s_pre=s_pre, j=j):
            if s_pre is not None:
                v.wait_ge(sS.h, s_pre)
            for g in range(NG):
                if g % 8 == 0:
                    v.wait_ge(tS.h, tmarks[g // 8])
                ps = self.ps0 if g < 16 else self.ps1
                ins = v.scalar_tensor_tensor(
                    self.s_run[:, g * 128:(g + 1) * 128],
                    self.s_run[:, g * 128:(g + 1) * 128],
                    self.cd_all[:, j * NG + g:j * NG + g + 1],
                    ps[:, (g % 16) * 128:(g % 16) * 128 + 128], Alu.mult, Alu.add)
            ins.then_inc(vS.h, 1)
        self.ve(scan)
        return v2, tmarks[-1], v1

    def p3a_states(self):
        prev = None
        for j in range(self.NCH):
            if prev is None:
                guards, tw, pf = None, (), None
            else:
                guards = [(self.qA.h, prev["qA"]), (self.sS.h, prev["sE"]),
                          (self.vS.h, prev["vN"])]
                tw = [(self.vS.h, prev["mv"]), (self.tS.h, prev["mk"])]
                pf = prev["scan"]
            ldtgt = self._chunk_loads(j, with_c=False, guards=guards)
            ttgt, cdt, e2s, cmk = self._chunk_common(j, ldtgt, tpose_waits=tw,
                                                     psum_free=pf)
            sst, smk, smv = self._states_scan(j, ttgt, cdt, e2s)
            prev = dict(cmk, mk=smk, mv=smv, scan=sst)
        vt = self.vS.n
        qt = self.qS.bump(32)
        self._contrib_qt = qt

        def contrib(sync, vt=vt):
            sync.wait_ge(self.vS.h, vt)
            sync.dma_start(out=self.ag_i.ap()[:, 0:DIN],
                           in_=self.s_run[:, :]).then_inc(self.qS.h, 16)
            sync.dma_start(out=self.ag_i.ap()[:, DIN:DIN + NG],
                           in_=self.p_run[:, :]).then_inc(self.qS.h, 16)
        self.sy(contrib)

    def p_ag_start(self):
        qS, cS = self.qS, self.cS
        qt = qS.n
        self._ag_c1 = cS.bump()

        def cc(g, qt=qt):
            g.wait_ge(qS.h, qt)
            g.collective_compute(
                "AllGather", Alu.bypass, replica_groups=[list(range(self.n_cores))],
                ins=[self.ag_i.ap()], outs=[self.ag_o.ap()]).then_inc(cS.h, 1)
        self.gp(cc)

    def p_ag_combine(self):
        qS, cS, vS = self.qS, self.cS, self.vS
        c1 = self._ag_c1
        sbuf = self.gv(16384, DIN, F32)
        dbuf = self.sml[:, 16:48]
        vz = vS.bump()

        def zer(v, cqt=self._contrib_qt):
            v.wait_ge(qS.h, cqt)
            v.memset(self.s_run[:, :], 0.0).then_inc(vS.h, 1)
        self.ve(zer)

        prev_v = vz
        for l in [l for l in [0, 1, 2, 4, 5, 6] if l < self.n_cores]:
            qt2 = qS.bump(32)

            def ldl(sync, l=l, c1=c1, prev_v=prev_v):
                sync.wait_ge(cS.h, c1)
                sync.wait_ge(vS.h, prev_v)
                sync.dma_start(out=sbuf, in_=self.ag_o.ap()[l * 128:(l + 1) * 128,
                                                            0:DIN]).then_inc(qS.h, 16)
                sync.dma_start(out=dbuf, in_=self.ag_o.ap()[l * 128:(l + 1) * 128,
                                                            DIN:DIN + NG]).then_inc(qS.h, 16)
            self.sy(ldl)
            vc = vS.bump()

            def comb(v, l=l, qt2=qt2):
                v.wait_ge(qS.h, qt2)
                pmc = self.pm_sb[:, 2 * l:2 * l + 1]
                pm1 = self.pm_sb[:, 2 * l + 1:2 * l + 2]
                v.scalar_tensor_tensor(dbuf, dbuf, pmc, pm1.broadcast_to([128, NG]),
                                       Alu.mult, Alu.add)
                v.tensor_scalar_mul(sbuf, sbuf, pmc)
                for g in range(NG):
                    ins = v.scalar_tensor_tensor(
                        self.s_run[:, g * 128:(g + 1) * 128],
                        self.s_run[:, g * 128:(g + 1) * 128],
                        dbuf[:, g:g + 1],
                        sbuf[:, g * 128:(g + 1) * 128], Alu.mult, Alu.add)
                ins.then_inc(vS.h, 1)
            self.ve(comb)
            prev_v = vc

    def p5m_ffn(self):
        S, SH = self.S, self.SH
        qS, vS, tS = self.qS, self.vS, self.tS
        wqb = [self.gv(0, 2048, BF16), self.gv(4096, 2048, BF16)]
        sf_sb = self.gv(8192, 2048, F32)
        stg = self.gv(16384, 2048, BF16)
        vt0 = vS.n
        t1 = tS.bump()

        def sfbc(t, vt0=vt0):
            t.wait_ge(vS.h, vt0)
            for (o, w) in self.SSL:
                ins = t.matmul(self.ps1[:, o:o + w], self.ones_f[64:65, :],
                               self.rows[64:65, 128 + o:128 + o + w],
                               start=True, stop=True)
            ins.then_inc(tS.h, 1)
        self.te(sfbc)
        v1 = vS.bump()

        def sfev(v, t1=t1):
            v.wait_ge(tS.h, t1)
            v.tensor_scalar(sf_sb[:, 0:S], self.ps1[:, 0:S], self.wsp[:, 5:6],
                            1.0 / 127.0, Alu.mult, Alu.mult).then_inc(vS.h, 1)
        self.ve(sfev)
        if self.debug_dump:
            qS.bump(16)
            vref = vS.n

            def dsf(sync, vref=vref):
                sync.wait_ge(vS.h, vref)
                sync.dma_start(out=self.dbg["dbg_sf"].ap(),
                               in_=sf_sb[:, 0:S]).then_inc(qS.h, 16)
            self.sy(dsf)

        mm_done = {}
        ev_done = {}
        st_done = {}
        for o in range(NKD):
            buf = wqb[o % 2]
            wprev = mm_done.get(o - 2)
            qw = qS.bump(16)

            def wld(sync, o=o, buf=buf, wprev=wprev):
                if wprev:
                    sync.wait_ge(tS.h, wprev)
                ap = AP(self.sp_wq, o * 128, [[DM, 128], [DM * 128, NKD], [1, 128]])
                sync.dma_start(out=buf.rearrange("p (k c) -> p k c", k=NKD),
                               in_=ap).then_inc(qS.h, 16)
            self.sy(wld)
            evp = ev_done.get(o - 1)
            tm = tS.bump()

            def mms(t, buf=buf, qw=qw, evp=evp, o=o, qm=tuple(self._p5q_marks)):
                t.wait_ge(qS.h, qw)
                if evp:
                    t.wait_ge(vS.h, evp)
                if o == 0:
                    t.wait_ge(vS.h, qm[NKD - 1])
                for k in range(NKD):
                    for (oo, w) in self.SSL:
                        ins = t.matmul(self.ps0[:, oo:oo + w],
                                       buf[:, k * 128:(k + 1) * 128],
                                       self.fsh[:, k * S + oo:k * S + oo + w],
                                       start=(k == 0), stop=(k == NKD - 1))
                ins.then_inc(tS.h, 1)
            self.te(mms)
            mm_done[o] = tm
            stp = st_done.get(o - 1)
            ve = vS.bump()

            def ev(v, tm=tm, v1=v1, stp=stp):
                v.wait_ge(tS.h, tm)
                v.wait_ge(vS.h, v1)
                if stp:
                    v.wait_ge(qS.h, stp)
                v.tensor_tensor(stg[:, 0:S], self.ps0[:, 0:S], sf_sb[:, 0:S],
                                Alu.mult).then_inc(vS.h, 1)
            self.ve(ev)
            ev_done[o] = ve
            st = qS.bump(16)

            def stf(sync, o=o, ve=ve):
                sync.wait_ge(vS.h, ve)
                sync.dma_start(out=self.sp_ffn.ap()[o], in_=stg[:, 0:S]).then_inc(qS.h, 16)
            self.sy(stf)
            st_done[o] = st

    def p3b_ssd(self):
        S, NCH = self.S, self.NCH
        qS, qA, vS, tS, sS = self.qS, self.qA, self.vS, self.tS, self.sS
        c_sl = self.gv(8448, 128, BF16)
        b_sl = self.gv(8192, 128, BF16)
        efl = self.gv(8704, 128, BF16)
        cbm = self.gv(8960, 128, BF16)
        tmp_b = self.gv(9216, 2048, BF16)
        trel = self.gv(13312, 2048, F32)
        yf = self.gv(21504, DIN, BF16)
        eflr = self.gv(13312, NH * 64, BF16)

        qi = qS.bump(16)

        def inde_ld(sync):
            sync.dma_start(out=self.f_drhs[0:64, :], in_=self.inde_h.ap()).then_inc(qS.h, 16)
        self.sy(inde_ld)

        prev_yst = None
        pv = None
        pv2 = None
        for j in range(NCH):
            csl = self.gv(8448, 128, BF16) if j % 2 == 0 else self.gv(29696, 128, BF16)
            if pv is None:
                guards = None
            else:
                tguard = max(pv["tC"], pv2["tY"]) if pv2 is not None else pv["tC"]
                guards = [(self.qA.h, pv["qA"]), (self.sS.h, pv["sE"]),
                          (self.vS.h, pv["vR"]), (self.tS.h, tguard)]
            ldtgt = self._chunk_loads(j, with_c=True, guards=guards, c_dst=csl)

            v1 = vS.bump()

            def rbuild(v, ldtgt=ldtgt):
                v.wait_ge(self.qS.h, ldtgt)
                v.tensor_copy(self.rhl[0:64, :], self.dacs_c[:, :])
                v.tensor_copy(self.d_lhA[0:64, :], self.cp_c[:, :])
                v.drain()
                v.tensor_sub(self.scr64[:, :], self.dacs_c[:, :], self.rhl[0:64, :])
                v.drain()
                v.tensor_copy(self.rhl[64:128, :], self.scr64[:, :])
                v.tensor_sub(self.scr64[:, :], self.cp_c[:, :], self.d_lhA[0:64, :])
                v.drain()
                v.tensor_copy(self.d_lhB[:, :], self.scr64[:, :]).then_inc(vS.h, 1)
            self.ve(rbuild)
            s1 = sS.bump()

            def ef_act(s, ldtgt=ldtgt):
                s.wait_ge(self.qS.h, ldtgt)
                s.activation(efl[0:64, :], self.dacs_c[:, :], Act.Exp).then_inc(sS.h, 1)
            self.se(ef_act)
            qb = qS.bump(48)

            def bounce_w(sync, j=j, v1=v1, s1=s1):
                sync.wait_ge(vS.h, v1)
                sync.wait_ge(sS.h, s1)
                sync.dma_start(out=self.sp_rr.ap()[j, 0].rearrange("(a b) -> a b", a=64),
                               in_=self.rhl[0:64, :]).then_inc(qS.h, 16)
                sync.dma_start(out=self.sp_rr.ap()[j, 1].rearrange("(a b) -> a b", a=64),
                               in_=self.rhl[64:128, :]).then_inc(qS.h, 16)
                sync.dma_start(out=self.sp_ee.ap()[j].rearrange("(a b) -> a b", a=64),
                               in_=efl[0:64, :]).then_inc(qS.h, 16)
            self.sy(bounce_w)
            qb2 = qS.bump(32)

            def bounce_r(sync, j=j, qb=qb, pv=pv):
                sync.wait_ge(qS.h, qb)
                if pv is not None:
                    sync.wait_ge(sS.h, pv["sD"])
                    sync.wait_ge(tS.h, pv["tD"])
                sync.dma_start(out=self.f_drhs[64:66, :],
                               in_=self.sp_rr.ap()[j]).then_inc(qS.h, 16)
                eap = AP(self.sp_ee, j * NH * 128, [[128, 2], [256, NG], [1, 128]])
                sync.dma_start(out=eflr[0:2, :], in_=eap).then_inc(qS.h, 16)
            self.sy(bounce_r)

            ttgt = self._chunk_tposes(
                j, ldtgt,
                tpose_waits=([(tS.h, pv["tY"])] if pv is not None else ()))

            vsc = vS.bump()

            def scast(v, prev_yst=prev_yst):
                if prev_yst:
                    v.wait_ge(self.qG.h, prev_yst)
                v.tensor_copy(self.f_epat, self.s_run[:, :]).then_inc(vS.h, 1)
            self.ve(scast)
            qp = qA.bump(16)

            def ptr(sc, vsc=vsc):
                sc.wait_ge(vS.h, vsc)
                sc.dma_start(out=self.f_prevs.rearrange("p (g c) -> p g c", g=NG),
                             in_=self.f_epat.rearrange("p (g c) -> p g c", g=NG),
                             transpose=True).then_inc(qA.h, 16)
            self.se(ptr)

            t1 = tS.bump()

            def cbmm(t, ldtgt=ldtgt, pv=pv, csl=csl):
                t.wait_ge(self.qS.h, ldtgt)
                if pv is not None:
                    t.wait_ge(vS.h, pv["vY"])
                t.matmul(self.ps1[:, 1792:1920], b_sl, csl,
                         start=True, stop=True).then_inc(tS.h, 1)
            self.te(cbmm)
            vcb = vS.bump()

            def cbev(v, t1=t1):
                v.wait_ge(tS.h, t1)
                v.tensor_tensor(cbm, self.ps1[:, 1792:1920], self.c_trl[:, :],
                                Alu.mult).then_inc(vS.h, 1)
            self.ve(cbev)

            for r in range(2):
                pvv = vS.n
                pvs = sS.n
                te = tS.bump()

                def epmm(t, r=r, qb2=qb2, qp=qp, pvv=pvv, pvs=pvs):
                    t.wait_ge(qS.h, qb2)
                    t.wait_ge(qA.h, qp)
                    t.wait_ge(vS.h, pvv)
                    for s in range(4):
                        ins = t.matmul(self.ps0[:, s * 512:(s + 1) * 512], self.sel_bf[:, :],
                                       eflr[0:2, r * 2048 + s * 512: r * 2048 + (s + 1) * 512],
                                       start=True, stop=True)
                    ins.then_inc(tS.h, 1)
                self.te(epmm)
                vev = vS.bump()

                def epev(v, r=r, te=te):
                    v.wait_ge(tS.h, te)
                    v.tensor_copy(self.f_epat[:, r * 2048:(r + 1) * 2048],
                                  self.ps0[:, :]).then_inc(vS.h, 1)
                self.ve(epev)

            if self.debug_dump and j == 0:
                qS.bump(32)

                def dep(sync, vcb=vcb, vev=vev):
                    sync.wait_ge(vS.h, vcb)
                    sync.wait_ge(vS.h, vev)
                    sync.dma_start(out=self.dbg["dbg_cbm"].ap(),
                                   in_=cbm).then_inc(qS.h, 16)
                    sync.dma_start(out=self.dbg["dbg_ep"].ap(),
                                   in_=self.f_epat[:, :]).then_inc(qS.h, 16)
                self.sy(dep)

            sst, _smk, _smv = self._states_scan(j, ttgt, None, None)

            for hf in range(2):
                go = hf * 16
                gdone = []
                for qq in range(2):
                    q = 2 * hf + qq
                    ps = self.ps0 if qq == 0 else self.ps1
                    pv_v = vS.n
                    pv_s = sS.n
                    td = tS.bump()

                    def dmm(t, q=q, ps=ps, pv_v=pv_v, pv_s=pv_s, sst=sst):
                        t.wait_ge(vS.h, max(pv_v, sst))
                        t.wait_ge(sS.h, pv_s)
                        for s in range(4):
                            c0 = q * 2048 + s * 512
                            t.matmul(ps[:, s * 512:(s + 1) * 512], self.d_lhA[:, :],
                                     self.f_drhs[:, c0:c0 + 512], start=True, stop=False)
                            ins = t.matmul(ps[:, s * 512:(s + 1) * 512], self.d_lhB[:, :],
                                           self.f_drhs[0:64, c0:c0 + 512],
                                           start=False, stop=True)
                        ins.then_inc(tS.h, 1)
                    self.te(dmm)
                    sd = sS.bump()

                    def dexp(s, qq=qq, ps=ps, td=td):
                        s.wait_ge(tS.h, td)
                        s.activation(trel, ps[:, :], Act.Relu, bias=30.0, scale=-1.0)
                        s.activation(self.f_gt[:, qq * 2048:(qq + 1) * 2048], trel,
                                     Act.Exp, bias=30.0, scale=-1.0).then_inc(sS.h, 1)
                    self.se(dexp)
                    vg = vS.bump()

                    def gmul(v, qq=qq, sd=sd, vcb=vcb):
                        v.wait_ge(sS.h, sd)
                        v.wait_ge(vS.h, vcb)
                        cb_b = cbm.unsqueeze(1).broadcast_to([128, 16, 128])
                        v.tensor_tensor(
                            self.f_gt[:, qq * 2048:(qq + 1) * 2048].rearrange(
                                "p (h c) -> p h c", h=16), cb_b,
                            self.f_gt[:, qq * 2048:(qq + 1) * 2048].rearrange(
                                "p (h c) -> p h c", h=16), Alu.mult).then_inc(vS.h, 1)
                    self.ve(gmul)
                    gdone.append(vg)

                if self.debug_dump and j == 0:
                    qS.bump(16)

                    def dgt(sync, hf=hf, gd=gdone[-1]):
                        sync.wait_ge(vS.h, gd)
                        sync.dma_start(out=self.dbg["dbg_gt"].ap()[hf],
                                       in_=self.f_gt[:, :]).then_inc(qS.h, 16)
                    self.sy(dgt)
                ty = tS.bump()

                def ymms(t, hf=hf, go=go, gd=gdone[-1], qp=qp, sst=sst, csl=csl):
                    t.wait_ge(vS.h, max(gd, sst))
                    t.wait_ge(qA.h, qp)
                    for gi in range(16):
                        g = go + gi
                        t.matmul(self.ps1[:, gi * 128:(gi + 1) * 128],
                                 self.f_prevs[:, g * 128:(g + 1) * 128],
                                 csl, start=True, stop=True)
                    for gi in range(16):
                        for r in range(2):
                            h = 2 * (go + gi) + r
                            hl = h - 32 * hf
                            ins = t.matmul(
                                self.ps0[r * 64:(r + 1) * 64, gi * 128:(gi + 1) * 128],
                                self.f_xtok[:, h * 64:(h + 1) * 64],
                                self.f_gt[:, hl * 128:(hl + 1) * 128],
                                start=True, stop=True)
                    ins.then_inc(tS.h, 1)
                self.te(ymms)
                vy = vS.bump()

                def yev(v, hf=hf, ty=ty):
                    v.wait_ge(tS.h, ty)
                    v.tensor_tensor(tmp_b, self.ps1[:, :],
                                    self.f_epat[:, hf * 2048:(hf + 1) * 2048], Alu.mult)
                    v.tensor_tensor(yf[:, hf * 2048:(hf + 1) * 2048], tmp_b,
                                    self.ps0[:, :], Alu.add).then_inc(vS.h, 1)
                self.ve(yev)

            yvt = self.vS.n
            qy = self.qG.bump(16)

            def yst(g, j=j, yvt=yvt):
                g.wait_ge(vS.h, yvt)
                g.dma_start(out=self.sp_y.ap()[j], in_=yf).then_inc(self.qG.h, 16)
            self.gp(yst)
            prev_yst = qy
            pv2 = pv
            pv = {"qA": ttgt, "sE": s1, "vR": v1, "tC": t1,
                  "sD": sS.n, "tD": tS.n, "tY": ty, "vY": vy}

    def p2b_gate(self):
        S, SH = self.S, self.SH
        qS, qG, vS, tS, sS = self.qS, self.qG, self.vS, self.tS, self.sS
        wb = [self.gv(0, 2048, BF16), self.gv(4096, 2048, BF16)]
        xcb = self.gv(8192, S, BF16)
        ytl = self.gv(8192 + 2 * S, S, BF16)
        zf = self.gv(8192 + 4 * S, S, F32)
        sg = self.gv(8192 + 8 * S, S, BF16)
        tb = self.gv(8192 + 10 * S, S, BF16)
        sq = self.gv(8192 + 12 * S, S, BF16)
        mm_done = {}
        gate_done = {}
        zev_done = {}
        yn_store = {}
        rms_q = []
        rms_tr = {}
        for m in range(NKI):
            buf = wb[m % 2]
            wprev = mm_done.get(m - 2)
            qw = qG.bump(16)

            def wld(g, m=m, buf=buf, wprev=wprev):
                if wprev:
                    g.wait_ge(tS.h, wprev)
                wap = AP(self.w_in_h, 128 * m, [[DPROJ, 128], [DPROJ * 128, NKD], [1, 128]])
                g.dma_start(out=buf.rearrange("p (k c) -> p k c", k=NKD),
                            in_=wap).then_inc(qG.h, 16)
            self.gp(wld)
            gprev = gate_done.get(m - 1)
            zprev = zev_done.get(m - 1)
            qt = qS.bump(32)

            def tld(sync, m=m, gprev=gprev):
                if gprev:
                    sync.wait_ge(vS.h, gprev)
                yap = AP(self.sp_y, m * 128, [[DIN, 128], [128 * DIN, self.NCH], [1, 128]])
                sync.dma_start(out=ytl[:, 0:self.NCH * 128].rearrange(
                    "p (j c) -> p j c", j=self.NCH), in_=yap).then_inc(qS.h, 16)
                sync.dma_start(out=xcb, in_=self.sp_xc.ap()[m]).then_inc(qS.h, 16)
            self.sy(tld)
            evp = gate_done.get(m - 1)
            tm = tS.bump()

            def mms(t, buf=buf, qw=qw, zprev=zprev):
                t.wait_ge(qG.h, qw)
                if zprev:
                    t.wait_ge(vS.h, zprev)
                for k in range(NKD):
                    for (o, w) in self.SSL:
                        ins = t.matmul(self.ps0[:, o:o + w],
                                       buf[:, k * 128:(k + 1) * 128],
                                       self.xnt[:, k * S + o: k * S + o + w],
                                       start=(k == 0), stop=(k == NKD - 1))
                ins.then_inc(tS.h, 1)
            self.te(mms)
            mm_done[m] = tm
            if m > 0:
                ps2, pm = rms_q.pop(0)
                tr = tS.bump()

                def rmsmm(t, m=pm, s2=ps2):
                    t.wait_ge(sS.h, s2)
                    for (o, w) in self.SSL:
                        ins = t.matmul(self.ps1[0:1, o:o + w], self.ones_bf[:, 0:1],
                                       sq[:, o:o + w],
                                       start=(m == 0), stop=(m == NKI - 1))
                    ins.then_inc(tS.h, 1)
                self.te(rmsmm)
                rms_tr[pm] = tr
            vz = vS.bump()

            def zev(v, tm=tm):
                v.wait_ge(tS.h, tm)
                v.tensor_copy(zf, self.ps0[:, 0:S]).then_inc(vS.h, 1)
            self.ve(zev)
            zev_done[m] = vz
            s1 = sS.bump()

            def sig(s, vz=vz):
                s.wait_ge(vS.h, vz)
                s.activation(sg, zf, Act.Sigmoid).then_inc(sS.h, 1)
            self.se(sig)
            ynp = yn_store.get(m - 1)
            vg = vS.bump()

            def gate(v, m=m, s1=s1, qt=qt, ynp=ynp):
                v.wait_ge(sS.h, s1)
                v.wait_ge(qS.h, qt)
                if ynp:
                    v.wait_ge(qS.h, ynp)
                v.tensor_tensor(zf, zf, sg, Alu.mult)
                v.scalar_tensor_tensor(tb, xcb, self.d_hp[:, m:m + 1], ytl, Alu.mult, Alu.add)
                v.tensor_tensor(tb, tb, zf, Alu.mult).then_inc(vS.h, 1)
            self.ve(gate)
            gate_done[m] = vg
            s2 = sS.bump()
            trp = rms_tr.get(m - 1)

            def sqs(s, vg=vg, trp=trp):
                s.wait_ge(vS.h, vg)
                if trp:
                    s.wait_ge(tS.h, trp)
                s.activation(sq, tb, Act.Square).then_inc(sS.h, 1)
            self.se(sqs)
            rms_q.append((s2, m))
            vy = vS.bump()

            def ynrm(v, m=m, s2=s2):
                v.wait_ge(sS.h, s2)
                v.tensor_scalar_mul(tb, tb, self.nw_sb[:, m:m + 1]).then_inc(vS.h, 1)
            self.ve(ynrm)
            qy = qS.bump(16)

            def ynst(sync, m=m, vy=vy):
                sync.wait_ge(vS.h, vy)
                sync.dma_start(out=self.sp_yn.ap()[m], in_=tb).then_inc(qS.h, 16)
            self.sy(ynst)
            yn_store[m] = qy

        ps2, pm = rms_q.pop(0)
        tS.bump()

        def rmslast(t, m=pm, s2=ps2):
            t.wait_ge(sS.h, s2)
            for (o, w) in self.SSL:
                ins = t.matmul(self.ps1[0:1, o:o + w], self.ones_bf[:, 0:1],
                               sq[:, o:o + w],
                               start=(m == 0), stop=(m == NKI - 1))
            ins.then_inc(tS.h, 1)
        self.te(rmslast)
        rt = self.tS.n
        vr = vS.bump()

        def rmsfin(v, rt=rt):
            v.wait_ge(tS.h, rt)
            v.tensor_copy(self.rows[32:33, 128:SH], self.ps1[0:1, 0:S]).then_inc(vS.h, 1)
        self.ve(rmsfin)

    def p4_outproj(self):
        S, SH = self.S, self.SH
        qS, qG, vS, tS = self.qS, self.qG, self.vS, self.tS
        TH = S // 2
        nsl = max(TH // 512, 1)
        ynh = self.fvw(0, NKI * TH)
        wb4 = [self.gv(0, 4096, BF16), self.gv(8192, 4096, BF16)]
        ffb = self.gv(16384, TH, BF16)
        xtb = self.gv(20480, TH, F32)
        a1 = self.gv(24576, 1024, F32)
        rstd_sb = self.gv(28672, 2048, F32)
        vt = self.vS.n
        t1 = tS.bump()

        def rbc(t, vt=vt):
            t.wait_ge(vS.h, vt)
            for (o, w) in self.SSL:
                ins = t.matmul(self.ps1[:, o:o + w], self.ones_f[32:33, :],
                               self.rows[32:33, 128 + o:128 + o + w],
                               start=True, stop=True)
            ins.then_inc(tS.h, 1)
        self.te(rbc)
        v1a = vS.bump()

        def rev(v, t1=t1):
            v.wait_ge(tS.h, t1)
            v.tensor_scalar(rstd_sb[:, 0:S], self.ps1[:, 0:S], 1.0 / DIN, EPS,
                            Alu.mult, Alu.add).then_inc(vS.h, 1)
        self.ve(rev)
        s1a = self.sS.bump()

        def rsq(s, v1a=v1a):
            s.wait_ge(vS.h, v1a)
            s.activation(rstd_sb[:, 0:S], rstd_sb[:, 0:S], Act.Sqrt).then_inc(self.sS.h, 1)
        self.se(rsq)
        v1 = vS.bump()

        def rrec(v, s1a=s1a):
            v.wait_ge(self.sS.h, s1a)
            v.reciprocal(rstd_sb[:, 0:S], rstd_sb[:, 0:S]).then_inc(vS.h, 1)
        self.ve(rrec)

        mm_done = {}
        ev_done = {}
        ost_done = {}
        for hf in range(2):
            tok0 = hf * TH
            ylds = []
            for k2 in range(NKI):
                wv = ev_done.get((hf - 1, NKD - 1)) if (hf == 1 and k2 <= 1) else None
                ctr = self.qA if k2 % 2 else qS
                qy = ctr.bump(16)

                def yld(eng, k2=k2, tok0=tok0, wv=wv, ctr=ctr):
                    if wv:
                        eng.wait_ge(vS.h, wv)
                    eng.dma_start(
                        out=ynh.rearrange("p (a c) -> p a c", a=NKI)[:, k2, :],
                        in_=self.sp_yn.ap()[k2, :, tok0:tok0 + TH]).then_inc(ctr.h, 16)
                (self.se if k2 % 2 else self.sy)(yld)
                ylds.append((ctr, qy))

            for o in range(NKD):
                buf = wb4[o % 2]
                key = (hf, o)
                pkey = (hf, o - 2) if o >= 2 else ((hf - 1, NKD + o - 2) if hf == 1 else None)
                wprev = mm_done.get(pkey) if pkey else None
                qw = qG.bump(16)

                def wld(g, o=o, buf=buf, wprev=wprev):
                    if wprev:
                        g.wait_ge(tS.h, wprev)
                    wap = AP(self.w_out_h, o * 128, [[DM, 128], [DM * 128, NKI], [1, 128]])
                    g.dma_start(out=buf.rearrange("p (k c) -> p k c", k=NKI),
                                in_=wap).then_inc(qG.h, 16)
                self.gp(wld)
                evp = ev_done.get((hf, o - 1))
                qf = qS.bump(32)

                def fld(sync, o=o, tok0=tok0, evp=evp):
                    if evp:
                        sync.wait_ge(vS.h, evp)
                    sync.dma_start(out=ffb[:, 0:TH],
                                   in_=self.sp_ffn.ap()[o, :, tok0:tok0 + TH]).then_inc(qS.h, 16)
                    sync.dma_start(out=xtb[:, 0:TH],
                                   in_=self.x_h.ap()[o * 128:(o + 1) * 128,
                                                     128 + tok0:128 + tok0 + TH]).then_inc(
                        qS.h, 16)
                self.sy(fld)
                tm = tS.bump()

                def mms(t, buf=buf, qw=qw, ylds=tuple(ylds), v1=v1, evp=evp, o=o):
                    t.wait_ge(qG.h, qw)
                    t.wait_ge(vS.h, v1)
                    if evp:
                        t.wait_ge(vS.h, evp)
                    if o == 0:
                        for ctr, qy in ylds:
                            t.wait_ge(ctr.h, qy)
                    for k in range(NKI):
                        for s in range(nsl):
                            w = min(512, TH - s * 512)
                            ins = t.matmul(self.ps0[:, s * 512:s * 512 + w],
                                           buf[:, k * 128:(k + 1) * 128],
                                           ynh.rearrange("p (a c) -> p a c", a=NKI)[
                                               :, k, s * 512:s * 512 + w],
                                           start=(k == 0), stop=(k == NKI - 1))
                    ins.then_inc(tS.h, 1)
                self.te(mms)
                mm_done[key] = tm
                osp = ost_done.get((hf, o - 1)) or (ost_done.get((0, NKD - 1))
                                                    if hf == 1 and o == 0 else None)
                ve = vS.bump()

                def evo(v, o=o, tok0=tok0, tm=tm, qf=qf, osp=osp):
                    v.wait_ge(tS.h, tm)
                    v.wait_ge(qS.h, qf)
                    if osp:
                        v.wait_ge(qS.h, osp)
                    v.tensor_tensor(a1[:, 0:TH], self.ps0[:, 0:TH],
                                    rstd_sb[:, tok0:tok0 + TH], Alu.mult)
                    v.tensor_tensor(a1[:, 0:TH], a1[:, 0:TH], ffb[:, 0:TH], Alu.add)
                    v.tensor_tensor(a1[:, 0:TH], a1[:, 0:TH], xtb[:, 0:TH],
                                    Alu.add).then_inc(vS.h, 1)
                self.ve(evo)
                ev_done[key] = ve
                qo = qS.bump(16)

                def ost(sync, o=o, tok0=tok0, ve=ve):
                    sync.wait_ge(vS.h, ve)
                    sync.dma_start(out=self.out_h.ap()[o * 128:(o + 1) * 128,
                                                       tok0:tok0 + TH],
                                   in_=a1[:, 0:TH]).then_inc(qS.h, 16)
                self.sy(ost)
                ost_done[key] = qo

        if self.debug_dump:
            qd = qS.bump(16 * 12)

            def dcp(sync):
                sync.dma_start(out=self.dbg["dbg_xc"].ap(), in_=self.sp_xc.ap()).then_inc(qS.h, 16)
                sync.dma_start(out=self.dbg["dbg_y"].ap(), in_=self.sp_y.ap()).then_inc(qS.h, 16)
                sync.dma_start(out=self.dbg["dbg_yn"].ap(), in_=self.sp_yn.ap()).then_inc(qS.h, 16)
                sync.dma_start(out=self.dbg["dbg_ffn"].ap(), in_=self.sp_ffn.ap()).then_inc(qS.h, 16)
                sync.dma_start(out=self.dbg["dbg_sr"].ap(), in_=self.ag_i.ap()).then_inc(qS.h, 16)
                sync.dma_start(out=self.dbg["dbg_wq"].ap(), in_=self.sp_wq.ap()).then_inc(qS.h, 16)
                sync.dma_start(out=self.dbg["dbg_xnf"].ap(), in_=self.sp_xn.ap()).then_inc(qS.h, 16)
                sync.dma_start(out=self.dbg["dbg_rows"].ap(), in_=self.rows[:, :]).then_inc(qS.h, 16)
                sync.dma_start(out=self.dbg["dbg_wsp"].ap(), in_=self.wsp[:, :]).then_inc(qS.h, 16)
                sync.dma_start(out=self.dbg["dbg_da"].ap(), in_=self.sp_da.ap()).then_inc(qS.h, 16)
                sync.dma_start(out=self.dbg["dbg_ee"].ap(), in_=self.sp_ee.ap()).then_inc(qS.h, 16)
                sync.dma_start(out=self.dbg["dbg_rr"].ap(), in_=self.sp_rr.ap()).then_inc(qS.h, 16)
            self.sy(dcp)
        fq = qS.n

        def fin(sync, fq=fq):
            sync.wait_ge(qS.h, fq)
        self.sy(fin)

    def build(self):
        self.emit_all()
        nc = self.nc
        with nc.Block() as block:
            @block.sync
            def _(e):
                e = _DmaChain(e, self.qS.h)
                for k, fn in self.ops:
                    if k == "sync":
                        fn(e)

            @block.scalar
            def _(e):
                e = _DmaChain(e, self.qA.h)
                for k, fn in self.ops:
                    if k == "scalar":
                        fn(e)

            @block.vector
            def _(e):
                for k, fn in self.ops:
                    if k == "vector":
                        fn(e)

            @block.tensor
            def _(e):
                for k, fn in self.ops:
                    if k == "tensor":
                        fn(e)

            @block.gpsimd
            def _(e):
                e = _DmaChain(e, self.qG.h)
                for k, fn in self.ops:
                    if k == "gpsimd":
                        fn(e)
        return nc


def _consts():
    ident = np.eye(128, dtype=np.float32)
    tril = np.triu(np.ones((128, 128))).astype(BF)
    inde = np.zeros((64, NH * 128), dtype=BF)
    for h in range(NH):
        inde[h, h * 128:(h + 1) * 128] = -1.0
    sel = np.zeros((2, 128), dtype=BF)
    sel[0, 0:64] = 1.0
    sel[1, 64:128] = 1.0
    return ident, tril, inde, sel


def _pm_mask(core):
    seq, pos = divmod(core, 4)
    pm = np.zeros((128, 16), dtype=np.float32)
    for l in range(8):
        lseq, lpos = divmod(l, 4)
        val = 1.0 if (lseq == seq and lpos < pos) else 0.0
        pm[:, 2 * l] = val
        pm[:, 2 * l + 1] = 1.0 - val
    return pm


def make_in_maps(inputs, S):
    x = np.asarray(inputs["L0_attn_hidden_states"])
    w_in = np.ascontiguousarray(np.asarray(inputs["L0_attn_W_in"], dtype=np.float32))
    conv_w = np.ascontiguousarray(np.asarray(inputs["L0_attn_conv_w"], dtype=np.float32))
    conv_b = np.ascontiguousarray(np.asarray(inputs["L0_attn_conv_b"], dtype=np.float32))
    a_vec = np.ascontiguousarray(np.asarray(inputs["L0_attn_A"], dtype=np.float32))
    d_vec = np.ascontiguousarray(np.asarray(inputs["L0_attn_D"], dtype=np.float32))
    dtb = np.ascontiguousarray(np.asarray(inputs["L0_attn_dt_bias"], dtype=np.float32))
    nw = np.ascontiguousarray(np.asarray(inputs["L0_attn_norm_w"], dtype=np.float32))
    w_out = np.ascontiguousarray(np.asarray(inputs["L0_attn_W_out"], dtype=np.float32))
    wq = np.ascontiguousarray(np.asarray(inputs["L0_ffn_weight"], dtype=np.float32).T)
    x = np.asarray(x, dtype=np.float32)
    ident, tril, inde, sel = _consts()
    nwp = np.ascontiguousarray(nw.reshape(NKI, 128).T)
    cbp = np.ascontiguousarray(conv_b.reshape(34, 128).T)
    cwp = np.ascontiguousarray(conv_w.reshape(34, 128, 4).transpose(1, 0, 2).reshape(128, 136))
    dpk = np.ascontiguousarray(np.stack([d_vec[0::2], d_vec[1::2]]))
    in_maps = []
    per = x.shape[1] // 4
    for c in range(8):
        seq, pos = divmod(c, 4)
        t0 = pos * per
        xs = x[seq, t0:t0 + S, :]
        if pos == 0:
            halo = np.zeros((128, DM), np.float32)
        else:
            halo = x[seq, t0 - 128:t0, :]
        xf = np.ascontiguousarray(np.concatenate([halo, xs], axis=0).T)
        in_maps.append({
            "x": xf, "w_in": w_in, "conv_w": conv_w, "conv_b": conv_b,
            "a_vec": a_vec, "d_vec": d_vec, "dt_bias": dtb, "norm_w": nw,
            "w_out": w_out, "wq_t": wq,
            "c_ident": ident, "c_tril": tril, "c_inde": inde, "c_sel": sel,
            "c_pm": _pm_mask(c), "c_nwp": nwp, "c_cbp": cbp, "c_cwp": cwp, "c_dpk": dpk,
        })
    return in_maps


def kernel(**inputs):
    S = 2048
    b = Builder(S)
    nc = b.build()
    in_maps = make_in_maps(inputs, S)
    res = run_bass_kernel_spmd(nc, in_maps, core_ids=list(range(8)))
    x = np.asarray(inputs["L0_attn_hidden_states"])
    out = np.empty(x.shape, dtype=np.float32)
    per = x.shape[1] // 4
    for c in range(8):
        seq, pos = divmod(c, 4)
        out[seq, pos * per:pos * per + S, :] = np.asarray(res.results[c]["out"]).T
    return out
